# version 01 (final)
# baseline (speedup 1.0000x reference)
import numpy as np
import ml_dtypes
import concourse.bass as bass
import concourse.mybir as mybir
from concourse.bass_utils import run_bass_kernel_spmd

F32 = mybir.dt.float32
BF16 = mybir.dt.bfloat16
AF = mybir.ActivationFunctionType
ALU = mybir.AluOpType

NCORE = 8
L, D, H = 2048, 1024, 16
TILE = 256
CH = 64
NTILE = L // TILE
NCH = TILE // CH
DH = 4096
EPS_LN = 1e-5
EPS_ON = 1e-5
EPS_QN = 1e-6
MAGIC = 12582912.0

ENGS = ("sp", "pe", "act", "dve", "gp")
DBG = False
WORK_SEMS = ("dma", "dmA", "dmB", "pe", "act", "dve", "gp")

(S_P1, S_Q1, S_P2, S_Q2, S_R1, S_R2, S_R3, S_P3,
 S_SA, S_SB, S_SC_, S_FD, S_SD, S_BE, S_QK, S_FG, S_SG) = range(17)
K_SC = 17
(C_A, C_B, C_ZV, C_ZUF, C_ZUS, C_G1, C_G2, C_QG) = range(8)
K_CF = 8
COEF_ORDER = [(0, C_A), (0, C_B), (1, C_A), (1, C_B),
              (0, C_ZV), (0, C_ZUF), (0, C_ZUS),
              (1, C_ZV), (1, C_ZUF), (1, C_ZUS),
              (0, C_G1), (1, C_G1), (0, C_G2), (1, C_G2),
              (0, C_QG), (1, C_QG)]


SEMS = {}


class Ctx:
    def __init__(self, mode, eng=None, ivars=None, cnt=None, pend=None):
        self.mode = mode
        self.e = eng
        self.iv = ivars or {}
        self.cnt = cnt
        self.pend = pend
        self.rec = []

    def i(self, ins, sem, n=1):
        if self.mode == "count":
            self.cnt[sem] += n
        elif ins is not None:
            ins.then_inc(SEMS[sem], n)
        return ins

    def s(self, sem, ins):
        self.i(ins, sem, 1)
        self.w(sem)
        return ins

    def w(self, sem, back=0, ahead=0):
        if self.mode == "count":
            self.rec.append((sem, self.cnt[sem] - back + ahead))
        else:
            s, v = self.pend.pop(0)
            assert s == sem, f"wait order mismatch {s} vs {sem}"
            if v > 0:
                self.e.wait_ge(SEMS[s], v)
            else:
                self.e.wait_ge(SEMS[s], 0)


class Sched:
    def __init__(self):
        self.items = []

    def op(self, engine, fn):
        self.items.append(("op", engine, fn))
        return fn

    def loop(self, name, n):
        sub = Sched()
        self.items.append(("loop", name, n, sub))
        return sub

    def barrier(self):
        self.items.append(("barrier", {}))

    def reset(self):
        self.items.append(("reset", {}))


OWN_SEM = {"act": "act", "dve": "dve", "gp": "gp"}


def _count(sched, cnt, barlin):
    for it in sched.items:
        if it[0] == "op":
            _, engine, fn = it
            c = Ctx("count", cnt=cnt)
            if engine in OWN_SEM:
                c.w(OWN_SEM[engine])
            fn(c)
            fn._waits = c.rec
        elif it[0] == "barrier":
            it[1]["snap"] = dict(cnt)
            barlin[0] += len(ENGS)
        elif it[0] == "reset":
            it[1]["snap"] = dict(cnt)
            for s in WORK_SEMS:
                cnt[s] = 0
        else:
            _, name, n, sub = it
            b0 = barlin[0]
            _count(sub, cnt, barlin)
            delta = barlin[0] - b0
            sub._bar_delta = delta
            barlin[0] = b0 + n * delta


def _emit_all(sched, nc, engs, ivars, barbase):
    bar = barbase
    for it in sched.items:
        if it[0] == "op":
            _, engine, fn = it
            c = Ctx(engine, engs[engine], ivars, pend=list(fn._waits))
            if engine in OWN_SEM:
                c.w(OWN_SEM[engine])
            fn(c)
            assert not c.pend
        elif it[0] == "barrier":
            for s in WORK_SEMS:
                if it[1]["snap"][s] > 0:
                    engs["sp"].wait_ge(SEMS[s], it[1]["snap"][s])
            bar = bar + len(ENGS)
            for en in ENGS:
                engs[en].sem_inc(SEMS["bar"], 1)
                engs[en].wait_ge(SEMS["bar"], bar)
        elif it[0] == "reset":
            for s in ("dma", "dmA", "dmB"):
                if it[1]["snap"][s] > 0:
                    engs["sp"].wait_ge(SEMS[s], it[1]["snap"][s])
            nc.all_engine_barrier()
            for s in WORK_SEMS:
                engs["sp"].sem_clear(SEMS[s])
            nc.all_engine_barrier()
        else:
            _, name, n, sub = it
            delta = sub._bar_delta
            with nc.Fori(0, n) as iv:
                ib = bar if delta == 0 else (bar + iv * delta)
                _emit_all(sub, nc, engs, {**ivars, name: iv}, ib)
            bar = bar + n * delta


def build_program(debug=False):
    global SEMS
    nc = bass.Bass("TRN2", target_bir_lowering=False)
    SEMS = {s: nc.alloc_semaphore(s) for s in ["dma", "dmA", "dmB", "pe", "act", "dve", "gp", "bar"]}

    ap = {}
    ap["xT"] = nc.dram_tensor("xT", [D, L], F32, kind="ExternalInput").ap()
    ap["outT"] = nc.dram_tensor("outT", [D, L], F32, kind="ExternalOutput").ap()
    for nm, shp in [
        ("Wq", [D, D]), ("Wk", [D, D]), ("Wv", [D, D]), ("Wg", [D, D]), ("Wo", [D, D]),
        ("w1T", [D, DH]), ("w2T", [DH, D]),
        ("Wb", [D, H]), ("Wfd", [D, H]), ("Wsd", [D, H]), ("Wfg", [D, H]), ("Wsg", [D, H]),
        ("bq", [D]), ("bk", [D]), ("bv", [D]), ("bg", [D]), ("bo", [D]),
        ("bb", [H]), ("bfd", [H]), ("fdb", [H]), ("bsd", [H]), ("sdb", [H]),
        ("bfg", [H]), ("bsg", [H]), ("onw", [64]),
        ("convw", [DH * 3]), ("convb", [DH]), ("rf", [1]),
    ]:
        ap[nm] = nc.dram_tensor(nm, shp, F32, kind="ExternalInput").ap()
    _ik = "ExternalOutput" if DBG else "Internal"
    ap["x1T"] = nc.dram_tensor("x1T", [D, L], F32, kind=_ik).ap()
    ap["hsT"] = nc.dram_tensor("hsT", [DH, L], BF16, kind=_ik).ap()
    ap["wbf"] = nc.dram_tensor("wbf", [5 * 8, 128 * 1024], BF16, kind="Internal").ap()
    if DBG:
        for nm, shp, dt_ in [
            ("dxn", [NTILE, 128, 8 * TILE], BF16), ("dpk", [NTILE, 128, 8 * TILE], F32),
            ("dpq", [NTILE, 128, 8 * TILE], F32), ("dpv", [NTILE, 128, 8 * TILE], F32),
            ("dpg", [NTILE, 128, 8 * TILE], F32), ("dsm", [NTILE, 16, 5 * TILE], F32),
            ("dnc", [NTILE, 16, 6 * TILE], F32), ("dsc", [NTILE, 16, TILE * K_SC], F32),
            ("dcf", [NTILE, 16, TILE * K_CF], F32), ("dcs", [NTILE, 16, TILE * K_CF], F32),
            ("dP", [NTILE, 128, 512], F32), ("dQ", [NTILE, 128, 512], F32),
            ("dz", [NTILE, 128, CH * 32], F32), ("dUF", [NTILE, 128, CH * 16], F32),
            ("dUS", [NTILE, 128, CH * 16], F32), ("dkl", [NTILE, 128, CH * 32], F32),
            ("dvc", [NTILE, 128, CH * 16], F32), ("dor", [NTILE, 1, CH * 64], BF16),
            ("dog", [NTILE, 128, NCH * 16 * CH], BF16),
            ("dP1", [NTILE, 128, 512], F32), ("dQ1", [NTILE, 128, 512], F32),
            ("dP2", [NTILE, 128, 512], F32), ("dQ2", [NTILE, 128, 512], F32),
            ("dP3", [NTILE, 128, 512], F32), ("dQ3", [NTILE, 128, 512], F32),
            ("dkl1", [NTILE, 128, CH * 32], F32), ("dvc1", [NTILE, 128, CH * 16], F32),
            ("dz1", [NTILE, 128, CH * 32], F32),
            ("dUF1", [NTILE, 128, CH * 16], F32), ("dUS1", [NTILE, 128, CH * 16], F32),
        ]:
            ap[nm] = nc.dram_tensor(nm, shp, dt_, kind="ExternalOutput").ap()

    um = np.zeros((128, 16), np.float32)
    bdm = np.zeros((128, 512), np.float32)
    for h in range(16):
        um[((h // 2) % 2) * 64:(((h // 2) % 2) + 1) * 64, h] = 1.0
        for r0 in (0, 32, 64):
            bdm[r0 + h, (h // 2) * 64:(h // 2 + 1) * 64] = 1.0
    hones = np.zeros((128, 8 * 16), np.float32)
    honesT = np.zeros((16, 8 * 128), np.float32)
    for cc in range(8):
        for slot in range(2):
            h = 2 * cc + slot
            hones[slot * 64:(slot + 1) * 64, cc * 16 + h] = 1.0
            honesT[h, cc * 128 + slot * 64:cc * 128 + (slot + 1) * 64] = 1.0
    cstv = [EPS_LN, EPS_QN, 1e-38, float(np.log(127.0)), -0.5, -1.0, 1.0 / 64,
            1.0, 0.0, 2.0, 1.0 / 1024, 1.0 / 127.0, MAGIC, -2.0, 64.0, -1e30]
    cd = {
        "cst": nc.inline_tensor(np.tile(np.asarray(cstv, np.float32), (128, 1)), name="c_cst").ap(),
        "ident": nc.inline_tensor(np.eye(128, dtype=np.float32), name="c_ident").ap(),
        "umask": nc.inline_tensor(um, name="c_umask").ap(),
        "bdmask": nc.inline_tensor(bdm.astype(np.float32), name="c_bdmask").ap(),
        "hones": nc.inline_tensor(hones, name="c_hones").ap(),
        "honesT": nc.inline_tensor(honesT, name="c_honesT").ap(),
        "onesc": nc.inline_tensor(np.ones((128, 1), np.float32), name="c_onesc").ap(),
        "onesr": nc.inline_tensor(np.ones((1, 128), np.float32), name="c_onesr").ap(),
    }

    prog = {"nc": nc, "cstv": cstv}
    _build(nc, ap, cd, prog, debug)
    return prog


def _build(nc, ap, cd, prog, debug):
    mgrs = []
    t = {}

    def sb(name, shape, dtype=F32):
        g = nc.sbuf_tensor(name, shape, dtype)
        t[name] = g.__enter__()
        mgrs.append(g)

    def psum(name, shape, dtype=F32):
        g = nc.psum_tensor(name, shape, dtype)
        t[name] = g.__enter__()
        mgrs.append(g)

    sb("cst", [128, 16])
    for j, v in enumerate(prog["cstv"]):
        nc.const_aps.register(F32, float(v), t["cst"][:, j:j + 1])
    sb("ident", [128, 128])
    sb("umask", [128, 16])
    sb("bdmask", [128, 512])
    sb("hones", [128, 8 * 16])
    sb("honesT", [16, 8 * 128])
    sb("onesc", [128, 1]); sb("onesr", [1, 128]); sb("onesrb", [1, 128], BF16)
    sb("rfc", [128, 8]); sb("rfr", [16, 8])
    sb("stage", [128, 2048])
    sb("wbuf", [128, 2 * 1024], BF16)
    sb("w_sm", [128, 5 * 8 * 16], BF16)
    sb("b_sm", [16, 8])
    sb("b_fm", [128, 8 * 5])
    sb("onwc", [128, 1])
    sb("cwv", [128, 32 * 3])
    sb("cbv", [128, 32])
    sb("xfm", [128, 8 * TILE])
    sb("xn", [128, 8 * TILE], BF16)
    sb("sqs", [128, TILE])
    sb("pj_k", [128, 8 * TILE]); sb("pj_q", [128, 8 * TILE])
    sb("pj_v", [128, 8 * TILE]); sb("pj_g", [128, 8 * TILE])
    sb("smalls", [16, 5 * TILE])
    sb("ncol", [16, 6 * TILE])
    sb("scol", [16, TILE * K_SC])
    sb("cfcol", [16, TILE * K_CF])
    sb("cscol", [16, TILE * K_CF])
    sb("klhs", [128, CH * 32])
    sb("ktr", [128, 22 * 128])
    sb("vcol", [128, CH * 16])
    sb("gsto", [128, CH * 16])
    sb("EP", [32, 512]); sb("EQ", [32, 512])
    sb("UFk", [128, 16]); sb("USk", [128, 16])
    sb("UFq", [128, CH * 16]); sb("USq", [128, CH * 16])
    sb("zsto", [128, CH * 32])
    sb("prods", [128, 80])
    sb("sc", [16, 16])
    sb("sg0", [16, 3]); sb("sg1", [16, 3])
    sb("ccol", [16, 2])
    sb("coefc", [16, 16])
    sb("crow", [1, 256])
    sb("orow", [1, CH * 64], BF16)
    sb("ztr", [128, 256])
    sb("zbd", [128, 1024])
    sb("st_P0", [128, 512]); sb("st_Q0", [128, 512])
    sb("st_P1", [128, 512]); sb("st_Q1", [128, 512])
    sb("Tst", [128, 1024])
    sb("ogst", [128, NCH * 16 * CH], BF16)
    sb("Bev", [128, 1024])
    sb("rows", [1, 2048])
    t["ffnx"] = t["zbd"]
    for i in range(8):
        psum(f"PS{i}", [128, 512])

    S = Sched()
    _ph_setup(S, nc, ap, cd, t)
    S.reset()
    outer = S.loop("ti", NTILE)
    _ph_tile(outer, nc, ap, t)
    _ph_ffn(S, nc, ap, t)

    cnt = {s: 0 for s in WORK_SEMS}
    barlin = [0]
    _count(S, cnt, barlin)

    engs = {"sp": nc.sync, "pe": nc.tensor, "act": nc.scalar,
            "dve": nc.vector, "gp": nc.gpsimd}
    _emit_all(S, nc, engs, {}, 0)

    for g in reversed(mgrs):
        g.__exit__(None, None, None)


def _ph_setup(S, nc, ap, cd, t):
    def sp_consts(c):
        e = c.e
        ins = None
        pairs = [
            (t["cst"][:], cd["cst"]),
            (t["ident"][:], cd["ident"]), (t["umask"][:], cd["umask"]),
            (t["bdmask"][:], cd["bdmask"]), (t["hones"][:], cd["hones"]),
            (t["honesT"][0:16, :], cd["honesT"]), (t["onesc"][:], cd["onesc"]),
            (t["onesr"][0:1, :], cd["onesr"]),
            (t["onwc"][0:64, 0:1], ap["onw"].rearrange("(a b) -> a b", b=1)),
            (t["onwc"][64:128, 0:1], ap["onw"].rearrange("(a b) -> a b", b=1)),
            (t["rows"][0:1, 0:1], ap["rf"].rearrange("(a b) -> a b", a=1)),
        ]
        for p_, d_ in pairs:
            c.i(e.dma_start(out=p_, in_=d_) if e else None, "dma", 16)
        c.w("dma")
    S.op("sp", sp_consts)
    S.barrier()

    rounds = []
    for j, nm in enumerate(("bq", "bk", "bv", "bg", "bo")):
        rounds.append((
            [(nm, None, 0, 1024)],
            [(0, 8, "PS6", j * 8)],
        ))
    rounds.append((
        [("bb", None, 0, 16), ("bfd", None, 16, 16), ("fdb", None, 32, 16),
         ("bsd", None, 48, 16), ("sdb", None, 64, 16), ("bfg", None, 80, 16),
         ("bsg", None, 96, 16)],
        [("smalls16", 7, "PS6", 40)],
    ))
    for seg in range(6):
        rounds.append((
            [("convw", seg, 0, 2048)],
            [("conv", seg, "PS7", 0)],
        ))
    for seg in range(2):
        rounds.append((
            [("convb", seg, 0, 2048)],
            [("convb", seg, "PS7", 96)],
        ))

    for ri, (loads, trs) in enumerate(rounds):
        def f_ld(c, loads=loads, ri=ri):
            if ri > 0:
                c.w("pe")
            for nm, seg, col0, n in loads:
                ins = None
                if c.e:
                    src = ap[nm].rearrange("(a b) -> a b", a=1) if seg is None else                         ap[nm].rearrange("(a b) -> a b", b=2048)[seg:seg + 1, :]
                    ins = c.e.dma_start(out=t["stage"][0:1, col0:col0 + n], in_=src)
                c.i(ins, "dma", 16)
            c.w("dma")
        S.op("sp", f_ld)

        def f_tr(c, trs=trs, ri=ri):
            c.w("dma")
            e = c.e
            ins = None
            if e:
                for tr in trs:
                    if tr[0] == "smalls16":
                        for j in range(tr[1]):
                            ins = e.transpose(t[tr[2]][0:16, 40 + j:41 + j],
                                              t["stage"][0:1, j * 16:(j + 1) * 16],
                                              t["ident"][0:1, 0:1])
                    elif tr[0] == "conv":
                        seg = tr[1]
                        tp, half = seg // 2, seg % 2
                        for k in range(16):
                            hc = half * 16 + k
                            ins = e.transpose(t["PS7"][:, hc * 3 + tp:hc * 3 + tp + 1],
                                              t["stage"][0:1, k * 128:(k + 1) * 128],
                                              t["ident"][0:1, 0:1])
                    elif tr[0] == "convb":
                        seg = tr[1]
                        for k in range(16):
                            hc = seg * 16 + k
                            ins = e.transpose(t["PS7"][:, 96 + hc:97 + hc],
                                              t["stage"][0:1, k * 128:(k + 1) * 128],
                                              t["ident"][0:1, 0:1])
                    else:
                        col0, nblk, psn, pc0 = tr
                        for k in range(nblk):
                            ins = e.transpose(t[psn][:, pc0 + k:pc0 + k + 1],
                                              t["stage"][0:1, col0 + k * 128:col0 + (k + 1) * 128],
                                              t["ident"][0:1, 0:1])
            c.i(ins, "pe", 1)
        S.op("pe", f_tr)

    def pe_rf(c):
        c.w("pe")
        c.i(c.e.matmul(t["PS3"][:, 0:1], t["onesr"][0:1, :], t["rows"][0:1, 0:1],
                       start=True, stop=True, skip_group_check=True) if c.e else None, "pe", 1)
    S.op("pe", pe_rf)
    S.barrier()

    def dve_t1(c):
        e = c.e
        c.i(e.tensor_copy(t["b_fm"][:].rearrange("p (c j) -> p c j", j=5),
                          t["PS6"][:, 0:40].rearrange("p (j c) -> p c j", c=8)) if e else None, "dve", 1)
        c.s("dve", e.tensor_copy(t["sc"][0:16, 0:7], t["PS6"][0:16, 40:47]) if e else None)
        c.i(e.tensor_copy(t["b_sm"][0:16, 0:1], t["sc"][0:16, 0:1]) if e else None, "dve", 1)
        c.i(e.tensor_add(t["b_sm"][0:16, 1:2], t["sc"][0:16, 1:2], t["sc"][0:16, 2:3]) if e else None, "dve", 1)
        c.i(e.tensor_add(t["b_sm"][0:16, 2:3], t["sc"][0:16, 3:4], t["sc"][0:16, 4:5]) if e else None, "dve", 1)
        c.i(e.tensor_copy(t["b_sm"][0:16, 3:4], t["sc"][0:16, 5:6]) if e else None, "dve", 1)
        c.i(e.tensor_copy(t["b_sm"][0:16, 4:5], t["sc"][0:16, 6:7]) if e else None, "dve", 1)
        c.i(e.tensor_copy(t["cwv"][:], t["PS7"][:, 0:96]) if e else None, "dve", 1)
        c.i(e.tensor_copy(t["cbv"][:], t["PS7"][:, 96:128]) if e else None, "dve", 1)
        c.s("dve", e.tensor_copy(t["rfc"][:, 0:1], t["PS3"][:, 0:1]) if e else None)
        c.i(e.tensor_scalar_add(t["rfc"][:, 1:2], t["rfc"][:, 0:1], 1.0) if e else None, "dve", 1)
        c.i(e.tensor_mul(t["rfc"][:, 2:3], t["rfc"][:, 0:1], t["rfc"][:, 0:1]) if e else None, "dve", 1)
        c.s("dve", e.tensor_scalar_mul(t["rfc"][:, 3:4], t["rfc"][:, 0:1], 2.0) if e else None)
        c.s("dve", e.tensor_scalar_add(t["rfc"][:, 4:5], t["rfc"][:, 2:3], 1.0) if e else None)
        c.i(e.tensor_copy(t["rfr"][0:16, 0:5], t["rfc"][0:16, 0:5]) if e else None, "dve", 1)
        c.i(e.tensor_copy(t["onesrb"][0:1, :], t["onesr"][0:1, :]) if e else None, "dve", 1)
    S.op("dve", dve_t1)

    def gp_zero(c):
        e = c.e
        for nm, apv in (("klhs", None), ("vcol", None), ("st_P0", None), ("st_Q0", None),
                        ("sg0", "h16"), ("prods", None), ("gsto", None)):
            tgt = t[nm][0:16, :] if apv == "h16" else t[nm][:]
            c.i(e.memset(tgt, 0.0) if e else None, "gp", 1)
    S.op("gp", gp_zero)

    def sp_smallw(c):
        e = c.e
        ins = None
        for wi, nm in enumerate(("Wb", "Wfd", "Wsd", "Wfg", "Wsg")):
            for dc in range(8):
                ins = None
                if e:
                    ins = e.dma_start(out=t["stage"][:, 1024 + (wi * 8 + dc) * 16:1024 + (wi * 8 + dc + 1) * 16],
                                      in_=ap[nm][dc * 128:(dc + 1) * 128, :])
                c.i(ins, "dma", 16)
        c.w("dma")
    S.op("sp", sp_smallw)
    S.barrier()

    def dve_smallw(c):
        e = c.e
        ins = None
        if e:
            ins = e.tensor_copy(t["w_sm"][:], t["stage"][:, 1024:1024 + 640])
        c.i(ins, "dve", 1)
    S.op("dve", dve_smallw)
    S.barrier()

    for wi, nm in enumerate(("Wq", "Wk", "Wv", "Wg", "Wo")):
        for dc in range(8):
            par = (wi * 8 + dc) % 2

            def f_ld(c, nm=nm, dc=dc, par=par, idx=wi * 8 + dc):
                if idx >= 2:
                    c.w("dve", back=1)
                ins = None
                if c.e:
                    ins = c.e.dma_start(out=t["stage"][:, par * 1024:(par + 1) * 1024],
                                        in_=ap[nm][dc * 128:(dc + 1) * 128, :])
                c.i(ins, "dma", 16)
            S.op("sp", f_ld)

            def f_cast(c, nm=nm, dc=dc, par=par, idx=wi * 8 + dc):
                c.w("dma")
                ins = None
                if c.e:
                    ins = c.e.tensor_copy(t["wbuf"][:, par * 1024:(par + 1) * 1024],
                                          t["stage"][:, par * 1024:(par + 1) * 1024])
                c.i(ins, "dve", 1)
            S.op("dve", f_cast)

            def f_out(c, nm=nm, dc=dc, par=par, idx=wi * 8 + dc):
                c.w("dve")
                ins = None
                if c.e:
                    ins = c.e.dma_start(out=ap["wbf"][idx:idx + 1, :],
                                        in_=t["wbuf"][:, par * 1024:(par + 1) * 1024])
                c.i(ins, "dma", 16)
            S.op("sp", f_out)

    def f_drain(c):
        c.w("dma"); c.w("dmA"); c.w("dmB")
    S.op("sp", f_drain)
    S.barrier()


def _ph_tile(S, nc, ap, t):
    def sp_x(c):
        for cc in range(8):
            ins = None
            if c.e:
                src = ap["xT"][cc * 128:(cc + 1) * 128, :].rearrange("p (n u) -> p n u", u=TILE)
                ins = c.e.dma_start(out=t["xfm"][:, cc * TILE:(cc + 1) * TILE],
                                    in_=src[:, c.iv["ti"], :])
            c.i(ins, "dma", 16)
    S.op("sp", sp_x)

    for cc in range(8):
        def f_sq(c, cc=cc):
            c.w("dma")
            c.w("pe")
            c.i(c.e.activation(t["sqs"][:], t["xfm"][:, cc * TILE:(cc + 1) * TILE], AF.Square) if c.e else None, "act", 1)
        S.op("act", f_sq)

        def f_mm(c, cc=cc):
            c.w("dma"); c.w("act")
            e = c.e
            ins = None
            if e:
                e.matmul(t["PS6"][0:1, 0:256], t["onesc"][:], t["xfm"][:, cc * TILE:(cc + 1) * TILE],
                         start=(cc == 0), stop=(cc == 7), skip_group_check=True)
                ins = e.matmul(t["PS6"][0:1, 256:512], t["onesc"][:], t["sqs"][:],
                               start=(cc == 0), stop=(cc == 7), skip_group_check=True)
            c.i(ins, "pe", 1)
        S.op("pe", f_mm)

    def f_lnrows(c):
        c.w("pe")
        e = c.e
        c.i(e.tensor_scalar_mul(t["rows"][0:1, 0:256], t["PS6"][0:1, 0:256], 1.0 / 1024) if e else None, "dve", 1)
        c.s("dve", e.tensor_scalar_mul(t["rows"][0:1, 256:512], t["PS6"][0:1, 256:512], 1.0 / 1024) if e else None)
        c.s("dve", e.tensor_mul(t["rows"][0:1, 512:768], t["rows"][0:1, 0:256], t["rows"][0:1, 0:256]) if e else None)
        c.i(e.tensor_sub(t["rows"][0:1, 768:1024], t["rows"][0:1, 256:512], t["rows"][0:1, 512:768]) if e else None, "dve", 1)
    S.op("dve", f_lnrows)

    def f_lnact(c):
        c.w("dve")
        e = c.e
        c.s("act", e.activation(t["rows"][0:1, 1024:1280], t["rows"][0:1, 768:1024], AF.Ln, bias=EPS_LN) if e else None)
        c.i(e.activation(t["rows"][0:1, 1280:1536], t["rows"][0:1, 1024:1280], AF.Exp, scale=-0.5) if e else None, "act", 1)
    S.op("act", f_lnact)

    def f_lnmul(c):
        c.w("act")
        c.i(c.e.tensor_mul(t["rows"][0:1, 1536:1792], t["rows"][0:1, 0:256], t["rows"][0:1, 1280:1536]) if c.e else None, "dve", 1)
    S.op("dve", f_lnmul)

    def f_lnbc(c):
        c.w("dve")
        c.i(c.e.matmul(t["PS7"][:, 0:512], t["onesr"][0:1, :], t["rows"][0:1, 1280:1792],
                       start=True, stop=True, skip_group_check=True) if c.e else None, "pe", 1)
    S.op("pe", f_lnbc)

    for cc in range(8):
        def f_x1(c, cc=cc):
            c.w("pe")
            c.w("gp")
            c.i(c.e.tensor_mul(t["sqs"][:], t["xfm"][:, cc * TILE:(cc + 1) * TILE], t["PS7"][:, 0:256]) if c.e else None, "dve", 1)
        S.op("dve", f_x1)

        def f_x2(c, cc=cc):
            c.w("dve")
            c.i(c.e.tensor_tensor(t["xn"][:, cc * TILE:(cc + 1) * TILE], t["sqs"][:],
                                  t["PS7"][:, 256:512], ALU.subtract) if c.e else None, "dve", 1)
        S.op("dve", f_x2)
    S.barrier()

    _ph_proj(S, nc, ap, t)
    S.barrier()
    _ph_normsmall(S, nc, ap, t)
    S.barrier()
    _ph_patterns(S, nc, ap, t)
    S.barrier()

    if DBG:
        _dbg_dump(S, ap, t, [("dxn", "xn"), ("dpk", "pj_k"), ("dpq", "pj_q"),
                             ("dpv", "pj_v"), ("dpg", "pj_g"), ("dsm", "smalls"),
                             ("dnc", "ncol"), ("dsc", "scol"), ("dcf", "cfcol"),
                             ("dcs", "cscol")])

    S.reset()
    for ci in range(NCH):
        _ph_chunk(S, nc, ap, t, ci)

    _ph_wo(S, nc, ap, t)
    S.reset()


def _ph_proj(S, nc, ap, t):
    dstmap = {0: "pj_q", 1: "pj_k", 2: "pj_v", 3: "pj_g"}
    seq = []
    for wi in range(4):
        for fc in range(8):
            for dc in range(8):
                seq.append((wi, fc, dc))

    for si, (wi, fc, dc) in enumerate(seq):
        par = si % 2

        def f_wld(c, wi=wi, fc=fc, dc=dc, par=par, si=si):
            if si >= 2:
                c.w("pe", back=1)
            ins = None
            if c.e:
                src = ap["wbf"][wi * 8 + dc:wi * 8 + dc + 1, :].rearrange("a (p f) -> (a p) f", p=128)[:, fc * 128:(fc + 1) * 128]
                ins = c.e.dma_start(out=t["wbuf"][:, par * 1024 + fc * 0:par * 1024 + 128], in_=src)
            c.i(ins, "dmA" if par == 0 else "dmB", 16)
        S.op("sp", f_wld)

        def f_mm(c, wi=wi, fc=fc, dc=dc, par=par, si=si):
            c.w("dmA" if par == 0 else "dmB")
            if dc == 0:
                c.w("act")
            ins = None
            if c.e:
                ins = c.e.matmul(t[f"PS{(wi * 8 + fc) % 2}"][:, 0:256],
                                 t["wbuf"][:, par * 1024:par * 1024 + 128],
                                 t["xn"][:, dc * TILE:(dc + 1) * TILE],
                                 start=(dc == 0), stop=(dc == 7), skip_group_check=True)
            c.i(ins, "pe", 1)
        S.op("pe", f_mm)

        if dc == 7:
            def f_ev(c, wi=wi, fc=fc, si=si):
                c.w("pe")
                ins = None
                if c.e:
                    bank = t[f"PS{(wi * 8 + fc) % 2}"]
                    bj = (0, 1, 2, 3)[wi]
                    if wi == 3:
                        ins = c.e.activation(t[dstmap[wi]][:, fc * TILE:(fc + 1) * TILE], bank[:, 0:256],
                                             AF.Identity, bias=t["b_fm"][:, fc * 5 + 3:fc * 5 + 4])
                    else:
                        ins = c.e.activation(t[dstmap[wi]][:, fc * TILE:(fc + 1) * TILE], bank[:, 0:256],
                                             AF.Silu, bias=t["b_fm"][:, fc * 5 + bj:fc * 5 + bj + 1])
                c.i(ins, "act", 1)
            S.op("act", f_ev)

    for wi in range(5):
        def f_mm(c, wi=wi):
            c.w("act")
            ins = None
            if c.e:
                for dc in range(8):
                    ins = c.e.matmul(t["PS2"][0:16, 0:256],
                                     t["w_sm"][:, (wi * 8 + dc) * 16:(wi * 8 + dc + 1) * 16],
                                     t["xn"][:, dc * TILE:(dc + 1) * TILE],
                                     start=(dc == 0), stop=(dc == 7), skip_group_check=True)
            c.i(ins, "pe", 1)
        S.op("pe", f_mm)

        def f_ev(c, wi=wi):
            c.w("pe")
            ins = None
            if c.e:
                ins = c.e.activation(t["smalls"][0:16, wi * TILE:(wi + 1) * TILE], t["PS2"][0:16, 0:256],
                                     AF.Sigmoid, bias=t["b_sm"][0:16, wi:wi + 1])
            c.i(ins, "act", 1)
        S.op("act", f_ev)


def _ph_normsmall(S, nc, ap, t):
    for j, (src, dstb, dsto) in enumerate((("pj_k", "PS3", 0), ("pj_q", "PS3", 256), ("pj_v", "PS4", 0))):
        for cc in range(8):
            def f_sq(c, src=src, cc=cc):
                c.w("pe")
                c.i(c.e.activation(t["sqs"][:], t[src][:, cc * TILE:(cc + 1) * TILE], AF.Square) if c.e else None, "act", 1)
            S.op("act", f_sq)

            def f_mm(c, src=src, dstb=dstb, dsto=dsto, cc=cc):
                c.w("act")
                c.i(c.e.matmul(t[dstb][0:16, dsto:dsto + 256], t["hones"][:, cc * 16:(cc + 1) * 16],
                               t["sqs"][:], start=(cc == 0), stop=(cc == 7), skip_group_check=True) if c.e else None, "pe", 1)
            S.op("pe", f_mm)

    def f_rstd(c):
        c.w("pe")
        e = c.e
        c.s("act", e.activation(t["ncol"][0:16, 5 * TILE:6 * TILE], t["PS3"][0:16, 0:256], AF.Ln, bias=EPS_QN) if e else None)
        c.s("act", e.activation(t["ncol"][0:16, 0:TILE], t["ncol"][0:16, 5 * TILE:6 * TILE], AF.Exp, scale=-0.5) if e else None)
        c.s("act", e.activation(t["ncol"][0:16, 5 * TILE:6 * TILE], t["PS3"][0:16, 256:512], AF.Ln, bias=EPS_QN) if e else None)
        c.i(e.activation(t["ncol"][0:16, TILE:2 * TILE], t["ncol"][0:16, 5 * TILE:6 * TILE], AF.Exp, scale=-0.5) if e else None, "act", 1)
    S.op("act", f_rstd)

    def f_nk(c):
        c.w("act")
        e = c.e
        c.s("dve", e.tensor_mul(t["ncol"][0:16, 5 * TILE:6 * TILE], t["ncol"][0:16, 0:TILE], t["ncol"][0:16, 0:TILE]) if e else None)
        c.i(e.tensor_mul(t["ncol"][0:16, 2 * TILE:3 * TILE], t["PS3"][0:16, 0:256], t["ncol"][0:16, 5 * TILE:6 * TILE]) if e else None, "dve", 1)
        c.i(e.tensor_copy(t["ncol"][0:16, 3 * TILE:4 * TILE], t["PS4"][0:16, 0:256]) if e else None, "dve", 1)
    S.op("dve", f_nk)

    for j, src in enumerate(("pj_k", "pj_q")):
        for cc in range(8):
            def f_bc(c, j=j, cc=cc):
                c.w("dve")
                if cc > 0 or j > 0:
                    c.w("dve")
                c.i(c.e.matmul(t["PS5"][:, 0:256], t["honesT"][0:16, cc * 128:(cc + 1) * 128],
                               t["ncol"][0:16, j * TILE:(j + 1) * TILE], start=True, stop=True, skip_group_check=True) if c.e else None, "pe", 1)
            S.op("pe", f_bc)

            def f_ap(c, j=j, src=src, cc=cc):
                c.w("pe")
                c.i(c.e.tensor_mul(t[src][:, cc * TILE:(cc + 1) * TILE],
                                   t[src][:, cc * TILE:(cc + 1) * TILE], t["PS5"][:, 0:256]) if c.e else None, "dve", 1)
            S.op("dve", f_ap)

    for cc in range(8):
        def f_qk1(c, cc=cc):
            c.w("dve")
            if cc > 0:
                c.w("pe")
            c.i(c.e.tensor_mul(t["sqs"][:], t["pj_k"][:, cc * TILE:(cc + 1) * TILE],
                               t["pj_q"][:, cc * TILE:(cc + 1) * TILE]) if c.e else None, "dve", 1)
        S.op("dve", f_qk1)

        def f_qk2(c, cc=cc):
            c.w("dve")
            c.i(c.e.matmul(t["PS4"][0:16, 256:512], t["hones"][:, cc * 16:(cc + 1) * 16], t["sqs"][:],
                           start=(cc == 0), stop=(cc == 7), skip_group_check=True) if c.e else None, "pe", 1)
        S.op("pe", f_qk2)

    def f_qk3(c):
        c.w("pe")
        c.i(c.e.tensor_copy(t["ncol"][0:16, 4 * TILE:5 * TILE], t["PS4"][0:16, 256:512]) if c.e else None, "dve", 1)
    S.op("dve", f_qk3)


def _dbg_dump(S, ap, t, pairs):
    def f_d(c, pairs=pairs):
        c.w("pe"); c.w("act"); c.w("dve"); c.w("gp")
        for dnm, snm in pairs:
            ins = None
            if c.e:
                src = t[snm]
                pdim, cdim = ap[dnm].shape[1], ap[dnm].shape[2]
                ins = c.e.dma_start(out=ap[dnm][0, 0:pdim, :],
                                    in_=src[0:pdim, 0:cdim])
            c.i(ins, "dma", 16)
        c.w("dma")
    S.op("sp", f_d)
    S.barrier()


def _ph_patterns(S, nc, ap, t):
    sm = lambda j: t["smalls"][0:16, j * TILE:(j + 1) * TILE]
    BE, FD, SD, FG, SG = sm(0), sm(1), sm(2), sm(3), sm(4)
    NK = t["ncol"][0:16, 2 * TILE:3 * TILE]
    NVV = t["ncol"][0:16, 3 * TILE:4 * TILE]
    QK = t["ncol"][0:16, 4 * TILE:5 * TILE]
    SC1 = t["ncol"][0:16, 5 * TILE:6 * TILE]

    def sco(j):
        return t["scol"][0:16, :].rearrange("p (u k) -> p u k", k=K_SC)[:, :, j]

    def cfo(j):
        return t["cfcol"][0:16, :].rearrange("p (u k) -> p u k", k=K_CF)[:, :, j]

    def cso(j):
        return t["cscol"][0:16, :].rearrange("p (u k) -> p u k", k=K_CF)[:, :, j]

    def f1(c):
        e = c.e
        c.i(e.tensor_mul(SC1, BE, NK) if e else None, "dve", 1)
        c.i(e.tensor_mul(sco(S_R1), BE, FD) if e else None, "dve", 1)
        c.s("dve", e.tensor_mul(sco(S_R2), BE, SD) if e else None)
        c.s("dve", e.tensor_scalar(sco(S_P3), SC1, -1.0, 1.0, ALU.mult, ALU.add) if e else None)
        c.s("dve", e.tensor_mul(sco(S_P1), sco(S_R1), sco(S_P3)) if e else None)
        c.i(e.tensor_scalar_mul(sco(S_P1), sco(S_P1), 2.0) if e else None, "dve", 1)
        c.s("dve", e.tensor_mul(sco(S_Q1), sco(S_R2), sco(S_P3)) if e else None)
        c.i(e.tensor_scalar_mul(sco(S_Q1), sco(S_Q1), 2.0) if e else None, "dve", 1)
        c.i(e.tensor_mul(sco(S_R1), sco(S_R1), sco(S_P3)) if e else None, "dve", 1)
        c.s("dve", e.tensor_mul(sco(S_R2), sco(S_R2), sco(S_P3)) if e else None)
        c.i(e.tensor_scalar(sco(S_P3), SC1, 1.0, -2.0, ALU.mult, ALU.add) if e else None, "dve", 1)
        c.i(e.tensor_mul(sco(S_SA), FD, FD) if e else None, "dve", 1)
        c.i(e.tensor_mul(sco(S_SB), SD, SD) if e else None, "dve", 1)
        c.s("dve", e.tensor_mul(sco(S_SC_), FD, SD) if e else None)
        c.s("dve", e.tensor_mul(sco(S_P2), sco(S_SA), BE) if e else None)
        c.i(e.tensor_mul(sco(S_P2), sco(S_P2), sco(S_P3)) if e else None, "dve", 1)
        c.s("dve", e.tensor_mul(sco(S_Q2), sco(S_SB), BE) if e else None)
        c.i(e.tensor_mul(sco(S_Q2), sco(S_Q2), sco(S_P3)) if e else None, "dve", 1)
        c.s("dve", e.tensor_mul(sco(S_R3), sco(S_SC_), BE) if e else None)
        c.s("dve", e.tensor_mul(sco(S_R3), sco(S_R3), sco(S_P3)) if e else None)
        c.s("dve", e.tensor_mul(sco(S_P3), BE, SC1) if e else None)
        c.i(e.tensor_mul(sco(S_P3), sco(S_P3), NVV) if e else None, "dve", 1)
    S.op("dve", f1)

    def f2(c):
        e = c.e
        c.i(e.tensor_copy(sco(S_FD), FD) if e else None, "gp", 1)
        c.i(e.tensor_copy(sco(S_SD), SD) if e else None, "gp", 1)
        c.i(e.tensor_copy(sco(S_BE), BE) if e else None, "gp", 1)
        c.i(e.tensor_copy(sco(S_QK), QK) if e else None, "gp", 1)
        c.i(e.tensor_copy(sco(S_FG), FG) if e else None, "gp", 1)
        c.i(e.tensor_copy(sco(S_SG), SG) if e else None, "gp", 1)
    S.op("gp", f2)

    def f3(c):
        c.w("dve"); c.w("gp")
        e = c.e
        rf1 = t["rfr"][0:16, 0:1] if e else None
        rfp1 = t["rfr"][0:16, 1:2] if e else None
        c.i(e.tensor_scalar_mul(cfo(C_B), SD, rf1) if e else None, "dve", 1)
        c.s("dve", e.tensor_scalar_mul(cfo(C_ZV), BE, rfp1) if e else None)
        c.i(e.tensor_copy(cso(C_ZV), cfo(C_ZV)) if e else None, "dve", 1)
        c.i(e.tensor_copy(cfo(C_A), FD) if e else None, "dve", 1)
        c.i(e.tensor_scalar_mul(cso(C_A), FD, rf1) if e else None, "dve", 1)
        c.s("dve", e.tensor_copy(cso(C_B), SD) if e else None)
        c.s("dve", e.tensor_scalar_mul(SC1, BE, -1.0) if e else None)
        c.i(e.tensor_mul(cfo(C_ZUF), SC1, FD) if e else None, "dve", 1)
        c.i(e.tensor_mul(cfo(C_ZUS), SC1, cfo(C_B)) if e else None, "dve", 1)
        c.i(e.tensor_mul(cso(C_ZUF), SC1, cso(C_A)) if e else None, "dve", 1)
        c.i(e.tensor_mul(cso(C_ZUS), SC1, SD) if e else None, "dve", 1)
        c.i(e.tensor_mul(cfo(C_G1), FD, FG) if e else None, "dve", 1)
        c.i(e.tensor_mul(cfo(C_G2), cfo(C_B), FG) if e else None, "dve", 1)
        c.i(e.tensor_mul(cfo(C_QG), QK, FG) if e else None, "dve", 1)
        c.i(e.tensor_mul(cso(C_G1), cso(C_A), SG) if e else None, "dve", 1)
        c.i(e.tensor_mul(cso(C_G2), SD, SG) if e else None, "dve", 1)
        c.i(e.tensor_mul(cso(C_QG), QK, SG) if e else None, "dve", 1)
    S.op("dve", f3)


def _ph_chunk(S, nc, ap, t, ci):
    _ph_build(S, nc, ap, t, ci)
    S.barrier()
    for s in range(CH):
        _ph_step(S, nc, ap, t, s, ci)
    S.barrier()
    if DBG and ci == 0:
        _dbg_dump(S, ap, t, [("dP", "st_P0"), ("dQ", "st_Q0"), ("dz", "zsto"),
                             ("dUF", "UFq"), ("dUS", "USq"), ("dkl", "klhs"),
                             ("dvc", "vcol"), ("dor", "orow")])
    if DBG and ci > 0:
        _dbg_dump(S, ap, t, [(f"dP{ci}", "st_P0"), (f"dQ{ci}", "st_Q0")])
    if DBG and ci == 1:
        _dbg_dump(S, ap, t, [("dkl1", "klhs"), ("dvc1", "vcol"), ("dz1", "zsto"),
                             ("dUF1", "UFq"), ("dUS1", "USq")])
    _ph_opost(S, nc, ap, t, ci)
    if DBG and ci == 3:
        _dbg_dump(S, ap, t, [("dog", "ogst")])
    S.reset()


def _ph_build(S, nc, ap, t, ci):
    def f_klhs(c):
        e = c.e
        for kq, src in ((0, "pj_k"), (1, "pj_q")):
            for slot in range(2):
                ins = None
                if e:
                    sh = slot * 64
                    srcv = t[src][sh:sh + 64, :] \
                        .rearrange("p (c8 n u) -> p c8 n u", c8=8, u=CH)[:, :, ci, :] \
                        .transpose([0, 2, 1])
                    dst = t["klhs"][sh:sh + 64, :] \
                        .rearrange("p (u k) -> p u k", k=32)[:, :, kq * 16 + slot:kq * 16 + 16:2]
                    ins = e.tensor_copy(dst, srcv)
                c.i(ins, "gp", 1)
    S.op("gp", f_klhs)

    def f_vg(c):
        e = c.e
        for nmsrc, nmdst in (("pj_v", "vcol"), ("pj_g", "gsto")):
            for slot in range(2):
                for par in range(2):
                    ins = None
                    if e:
                        sh = slot * 64
                        srcv = t[nmsrc][sh:sh + 64, :] \
                            .rearrange("p (c8 n u) -> p c8 n u", c8=8, u=CH)[:, par:8:2, ci, :] \
                            .transpose([0, 2, 1])
                        col0 = 2 * (2 * 0 + par) + slot
                        dst = t[nmdst][par * 64:(par + 1) * 64, :] \
                            .rearrange("p (u k) -> p u k", k=16)[:, :, col0:16:4]
                        ins = e.tensor_copy(dst, srcv)
                    c.i(ins, "gp", 1)
    S.op("gp", f_vg)

    for grp in range(22):
        n_in = min(3, CH - grp * 3)
        if n_in <= 0:
            break

        def f_tr(c, grp=grp, n_in=n_in):
            c.w("gp", back=8)
            if grp > 0:
                c.w("act")
            e = c.e
            ins = None
            if e:
                for j in range(n_in):
                    s = grp * 3 + j
                    ins = e.transpose(t["PS2"][0:16, j * 128:(j + 1) * 128],
                                      t["klhs"][:, s * 32:s * 32 + 16], t["ident"][:])
            c.i(ins, "pe", 1)
        S.op("pe", f_tr)

        def f_ev(c, grp=grp, n_in=n_in):
            c.w("pe")
            for j in range(n_in):
                c.i(c.e.copy(t["ktr"][j * 32:j * 32 + 16, grp * 128:(grp + 1) * 128],
                             t["PS2"][0:16, j * 128:(j + 1) * 128]) if c.e else None, "act", 1)
        S.op("act", f_ev)


def _ph_step(S, nc, ap, t, s, ci):
    par = s % 2
    P = t["st_P0"] if par == 0 else t["st_P1"]
    Q = t["st_Q0"] if par == 0 else t["st_Q1"]
    Pn = t["st_P1"] if par == 0 else t["st_P0"]
    Qn = t["st_Q1"] if par == 0 else t["st_Q0"]
    sgin = t["sg0"] if par == 0 else t["sg1"]
    sgout = t["sg1"] if par == 0 else t["sg0"]
    tok = ci * CH + s

    def sco(j, w=1):
        return t["scol"][0:16, tok * K_SC + j:tok * K_SC + j + w]

    def cfo(j, w=1):
        return t["cfcol"][0:16, tok * K_CF + j:tok * K_CF + j + w]

    def cso(j, w=1):
        return t["cscol"][0:16, tok * K_CF + j:tok * K_CF + j + w]

    vs = lambda: t["vcol"][:, s * 16:(s + 1) * 16]

    def f_umm(c, s=s):
        c.w("dve"); c.w("gp")
        e = c.e
        ins = None
        if e:
            lhs = t["klhs"][:, s * 32:(s + 1) * 32]
            e.matmul(t["PS0"][0:32, 0:512], lhs, P[:], start=True, stop=True, skip_group_check=True)
            ins = e.matmul(t["PS1"][0:32, 0:512], lhs, Q[:], start=True, stop=True, skip_group_check=True)
        c.i(ins, "pe", 1)
    S.op("pe", f_umm)

    def f_evP(c, s=s):
        c.w("pe")
        c.i(c.e.copy(t["EP"][0:32, :], t["PS0"][0:32, 0:512]) if c.e else None, "act", 1)
    S.op("act", f_evP)

    def f_evQ(c, s=s):
        c.w("pe")
        c.i(c.e.tensor_copy(t["EQ"][0:32, :], t["PS1"][0:32, 0:512]) if c.e else None, "dve", 1)
    S.op("dve", f_evQ)

    def f_tr(c, s=s):
        c.w("act"); c.w("dve")
        e = c.e
        ins = None
        if e:
            for b in range(4):
                ins = e.transpose(t["PS2"][:, b * 36:b * 36 + 32], t["EP"][0:32, b * 128:(b + 1) * 128], t["ident"][0:32, 0:32])
            for b in range(4):
                ins = e.transpose(t["PS3"][:, b * 36:b * 36 + 32], t["EQ"][0:32, b * 128:(b + 1) * 128], t["ident"][0:32, 0:32])
        c.i(ins, "pe", 1)
    S.op("pe", f_tr)

    def uc_ap(ps, kq):
        return t[ps][:, 0:160].rearrange("p (b q) -> p b q", q=40)[:, :, kq * 16:kq * 16 + 4]

    def f_uc(c, s=s):
        c.w("pe")
        e = c.e
        ins = None
        um4 = t["umask"][:].rearrange("p (b q) -> p b q", q=4)
        c.i(e.tensor_mul(t["UFk"][:].rearrange("p (b q) -> p b q", q=4), uc_ap("PS2", 0), um4) if e else None, "dve", 1)
        c.i(e.tensor_mul(t["USk"][:].rearrange("p (b q) -> p b q", q=4), uc_ap("PS3", 0), um4) if e else None, "dve", 1)
        c.i(e.tensor_mul(t["UFq"][:, s * 16:(s + 1) * 16].rearrange("p (b q) -> p b q", q=4), uc_ap("PS2", 1), um4) if e else None, "dve", 1)
        c.i(e.tensor_mul(t["USq"][:, s * 16:(s + 1) * 16].rearrange("p (b q) -> p b q", q=4), uc_ap("PS3", 1), um4) if e else None, "dve", 1)
    S.op("dve", f_uc)

    def f_pr1(c, s=s):
        c.w("dve")
        e = c.e
        c.i(e.tensor_mul(t["prods"][:, 0:16], t["UFk"][:], vs()) if e else None, "dve", 1)
        c.i(e.tensor_mul(t["prods"][:, 32:48], t["UFk"][:], t["UFk"][:]) if e else None, "dve", 1)
        c.i(e.tensor_mul(t["prods"][:, 64:80], t["UFk"][:], t["USk"][:]) if e else None, "dve", 1)
    S.op("dve", f_pr1)

    def f_pr2(c, s=s):
        c.w("dve", back=3)
        e = c.e
        c.i(e.tensor_mul(t["prods"][:, 16:32], t["USk"][:], vs()) if e else None, "gp", 1)
        c.i(e.tensor_mul(t["prods"][:, 48:64], t["USk"][:], t["USk"][:]) if e else None, "gp", 1)
    S.op("gp", f_pr2)

    def f_dots(c, s=s):
        c.w("dve"); c.w("gp")
        e = c.e
        ins = None
        if e:
            for k in range(5):
                ins = e.matmul(t["PS6"][0:16, k:k + 1], t["prods"][:, k * 16:(k + 1) * 16],
                               t["onesc"][:], start=True, stop=True, skip_group_check=True)
        c.i(ins, "pe", 1)
    S.op("pe", f_dots)

    def f_alg(c, s=s):
        c.w("pe")
        e = c.e
        sc = t["sc"]
        D_ = t["PS6"]
        c.i(e.tensor_mul(sc[0:16, 8:10], D_[0:16, 0:2], sco(S_P1, 2)) if e else None, "dve", 1)
        c.s("dve", e.tensor_mul(sc[0:16, 10:12], D_[0:16, 2:4], sco(S_P2, 2)) if e else None)
        c.i(e.tensor_add(sc[0:16, 0:2], sc[0:16, 8:10], sc[0:16, 10:12]) if e else None, "dve", 1)
        c.i(e.tensor_mul(sc[0:16, 12:14], D_[0:16, 0:2], sco(S_R1, 2)) if e else None, "dve", 1)
        c.s("dve", e.tensor_mul(sc[0:16, 14:15], D_[0:16, 4:5], sco(S_R3)) if e else None)
        c.s("dve", e.tensor_add(sc[0:16, 2:3], sc[0:16, 12:13], sc[0:16, 13:14]) if e else None)
        c.i(e.tensor_add(sc[0:16, 2:3], sc[0:16, 2:3], sc[0:16, 14:15]) if e else None, "dve", 1)
        c.s("dve", e.tensor_mul(sc[0:16, 8:11], sgin[0:16, 0:3], sco(S_SA, 3)) if e else None)
        c.i(e.tensor_add(sc[0:16, 0:3], sc[0:16, 0:3], sc[0:16, 8:11]) if e else None, "dve", 1)
        c.s("dve", e.tensor_copy(sc[0:16, 11:12], sco(S_P3)) if e else None)
        c.i(e.tensor_add(sc[0:16, 0:1], sc[0:16, 0:1], sc[0:16, 11:12]) if e else None, "dve", 1)
        c.i(e.tensor_add(sc[0:16, 1:2], sc[0:16, 1:2], sc[0:16, 11:12]) if e else None, "dve", 1)
        c.i(e.tensor_add(sc[0:16, 2:3], sc[0:16, 2:3], sc[0:16, 11:12]) if e else None, "dve", 1)
    S.op("dve", f_alg)

    def f_mix(c, s=s):
        c.w("dve")
        e = c.e
        sc = t["sc"]
        rf1 = t["rfr"][0:16, 0:1] if e else None
        rf2 = t["rfr"][0:16, 2:3] if e else None
        rf3 = t["rfr"][0:16, 3:4] if e else None
        rf4 = t["rfr"][0:16, 4:5] if e else None
        c.i(e.tensor_scalar_mul(sc[0:16, 10:11], sc[0:16, 1:2], rf2) if e else None, "dve", 1)
        c.i(e.tensor_scalar_mul(sc[0:16, 11:12], sc[0:16, 0:1], rf2) if e else None, "dve", 1)
        c.s("dve", e.tensor_scalar_mul(sc[0:16, 12:13], sc[0:16, 2:3], rf3) if e else None)
        c.s("dve", e.tensor_add(sc[0:16, 4:5], sc[0:16, 0:1], sc[0:16, 10:11]) if e else None)
        c.i(e.tensor_add(sc[0:16, 4:5], sc[0:16, 4:5], sc[0:16, 12:13]) if e else None, "dve", 1)
        c.s("dve", e.tensor_add(sc[0:16, 5:6], sc[0:16, 1:2], sc[0:16, 11:12]) if e else None)
        c.i(e.tensor_add(sc[0:16, 5:6], sc[0:16, 5:6], sc[0:16, 12:13]) if e else None, "dve", 1)
        c.s("dve", e.tensor_add(sc[0:16, 13:14], sc[0:16, 0:1], sc[0:16, 1:2]) if e else None)
        c.i(e.tensor_scalar_mul(sc[0:16, 13:14], sc[0:16, 13:14], rf1) if e else None, "dve", 1)
        c.s("dve", e.tensor_scalar_mul(sc[0:16, 14:15], sc[0:16, 2:3], rf4) if e else None)
        c.i(e.tensor_add(sc[0:16, 6:7], sc[0:16, 13:14], sc[0:16, 14:15]) if e else None, "dve", 1)
        c.i(e.tensor_scalar_max(sc[0:16, 4:6], sc[0:16, 4:6], 0.0) if e else None, "dve", 1)
    S.op("dve", f_mix)

    def f_c1(c, s=s):
        c.w("dve")
        e = c.e
        c.s("act", e.activation(t["sc"][0:16, 8:10], t["sc"][0:16, 4:6], AF.Ln, bias=1e-38) if e else None)
        c.i(e.activation(t["ccol"][0:16, 0:2], t["sc"][0:16, 8:10], AF.Exp, scale=-0.5) if e else None, "act", 1)
    S.op("act", f_c1)

    def f_c2(c, s=s):
        c.w("act")
        e = c.e
        c.s("dve", e.tensor_scalar(t["ccol"][0:16, 0:2], t["ccol"][0:16, 0:2], 64.0, 1.0, ALU.mult, ALU.min) if e else None)
        c.i(e.tensor_mul(t["sc"][0:16, 8:10], t["ccol"][0:16, 0:2], t["ccol"][0:16, 0:2]) if e else None, "dve", 1)
        c.s("dve", e.tensor_mul(t["sc"][0:16, 10:11], t["ccol"][0:16, 0:1], t["ccol"][0:16, 1:2]) if e else None)
        c.i(e.tensor_mul(sgout[0:16, 0:3], t["sc"][0:16, 8:11], t["sc"][0:16, 4:7]) if e else None, "dve", 1)
    S.op("dve", f_c2)

    def f_coef(c, s=s):
        c.w("dve")
        e = c.e
        cfv = t["ccol"][0:16, 0:1].broadcast_to([16, K_CF - 1]) if e else None
        csv = t["ccol"][0:16, 1:2].broadcast_to([16, K_CF - 1]) if e else None
        c.i(e.tensor_mul(t["coefc"][0:16, 0:7], cfv, cfo(0, K_CF - 1)) if e else None, "gp", 1)
        c.i(e.tensor_mul(t["coefc"][0:16, 8:15], csv, cso(0, K_CF - 1)) if e else None, "gp", 1)
        c.i(e.tensor_copy(t["coefc"][0:16, 7:8], cfo(C_QG)) if e else None, "gp", 1)
        c.i(e.tensor_copy(t["coefc"][0:16, 15:16], cso(C_QG)) if e else None, "gp", 1)
    S.op("gp", f_coef)

    def f_ctr(c, s=s):
        c.w("gp")
        e = c.e
        ins = None
        if e:
            for j, (grp, slot) in enumerate(COEF_ORDER):
                ins = e.transpose(t["PS6"][0:1, 64 + j * 16:64 + (j + 1) * 16],
                                  t["coefc"][0:16, grp * 8 + slot:grp * 8 + slot + 1],
                                  t["ident"][0:16, 0:16])
        c.i(ins, "pe", 1)
    S.op("pe", f_ctr)

    def f_cev(c, s=s):
        c.w("pe")
        e = c.e
        c.i(e.copy(t["crow"][0:1, 0:224], t["PS6"][0:1, 64:288]) if e else None, "act", 1)
        c.i(e.copy(t["orow"][0:1, s * 64 + 32:s * 64 + 64], t["PS6"][0:1, 288:320]) if e else None, "act", 1)
    S.op("act", f_cev)

    def f_orow(c, s=s):
        c.w("act")
        e = c.e
        ins = None
        if e:
            a = t["crow"][0:1, 160:224].rearrange("p (g x) -> p g x", x=16)[:, 0:4:2, :]
            b = t["crow"][0:1, 160:224].rearrange("p (g x) -> p g x", x=16)[:, 1:4:2, :]
            ins = e.tensor_tensor(t["orow"][0:1, s * 64:s * 64 + 32].rearrange("p (g x) -> p g x", x=16),
                                  a, b, ALU.add)
        c.i(ins, "gp", 1)
    S.op("gp", f_orow)

    def f_bc(c, s=s):
        c.w("act")
        c.i(c.e.matmul(t["PS7"][:, 0:160], t["onesr"][0:1, :], t["crow"][0:1, 0:160],
                       start=True, stop=True, skip_group_check=True) if c.e else None, "pe", 1)
    S.op("pe", f_bc)

    def f_z1(c, s=s):
        c.w("pe")
        e = c.e
        ins = None
        B = t["PS7"]
        zf = t["zsto"][:, s * 32:s * 32 + 16]
        c.i(e.tensor_mul(zf, vs(), B[:, 64:80]) if e else None, "dve", 1)
        c.s("dve", e.tensor_mul(t["Tst"][:, 0:16], t["UFk"][:], B[:, 80:96]) if e else None)
        c.s("dve", e.tensor_add(zf, zf, t["Tst"][:, 0:16]) if e else None)
        c.s("dve", e.tensor_mul(t["Tst"][:, 0:16], t["USk"][:], B[:, 96:112]) if e else None)
        c.i(e.tensor_add(zf, zf, t["Tst"][:, 0:16]) if e else None, "dve", 1)
    S.op("dve", f_z1)

    def f_z2(c, s=s):
        c.w("pe", back=0)
        e = c.e
        ins = None
        B = t["PS7"]
        zs = t["zsto"][:, s * 32 + 16:s * 32 + 32]
        c.i(e.tensor_mul(zs, vs(), B[:, 112:128]) if e else None, "dve", 1)
        c.s("dve", e.tensor_mul(t["Tst"][:, 16:32], t["UFk"][:], B[:, 128:144]) if e else None)
        c.s("dve", e.tensor_add(zs, zs, t["Tst"][:, 16:32]) if e else None)
        c.s("dve", e.tensor_mul(t["Tst"][:, 16:32], t["USk"][:], B[:, 144:160]) if e else None)
        c.i(e.tensor_add(zs, zs, t["Tst"][:, 16:32]) if e else None, "dve", 1)
    S.op("dve", f_z2)

    def f_ztr(c, s=s):
        c.w("dve")
        e = c.e
        ins = None
        if e:
            ins = e.transpose(t["PS6"][0:16, 320:448], t["zsto"][:, s * 32:s * 32 + 16], t["ident"][:])
            ins = e.transpose(t["PS5"][0:16, 0:128], t["zsto"][:, s * 32 + 16:s * 32 + 32], t["ident"][:])
        c.i(ins, "pe", 1)
    S.op("pe", f_ztr)

    def f_zev(c, s=s):
        c.w("pe")
        e = c.e
        ins = None
        r0 = (s % 3) * 32
        c.i(e.tensor_copy(t["ztr"][r0:r0 + 16, 0:128], t["PS6"][0:16, 320:448]) if e else None, "dve", 1)
        c.i(e.tensor_copy(t["ztr"][r0:r0 + 16, 128:256], t["PS5"][0:16, 0:128]) if e else None, "dve", 1)
    S.op("dve", f_zev)

    def f_zbd(c, s=s):
        c.w("dve")
        e = c.e
        ins = None
        r0 = (s % 3) * 32
        if e:
            src = t["ztr"][r0:r0 + 16, :].rearrange("p (fs x) -> p fs x", fs=2) \
                .unsqueeze(2).broadcast_to([16, 2, 4, 128])
            msk = t["bdmask"][r0:r0 + 16, :].rearrange("p (g x) -> p g x", g=4) \
                .unsqueeze(1).broadcast_to([16, 2, 4, 128])
            dst = t["zbd"][r0:r0 + 16, :].rearrange("p (fs g x) -> p fs g x", fs=2, g=4)
            ins = e.tensor_mul(dst, src, msk)
        c.i(ins, "dve", 1)
    S.op("dve", f_zbd)

    def f_inj(c, s=s):
        c.w("dve")
        e = c.e
        ins = None
        if e:
            r0 = (s % 3) * 32
            lhs = t["ktr"][r0:r0 + 16, (s // 3) * 128:(s // 3 + 1) * 128]
            ins = e.matmul(t["PS4"][:, 0:512], lhs, t["zbd"][r0:r0 + 16, 0:512], start=True, stop=True, skip_group_check=True)
            ins = e.matmul(t["PS5"][:, 0:512], lhs, t["zbd"][r0:r0 + 16, 512:1024], start=True, stop=True, skip_group_check=True)
        c.i(ins, "pe", 1)
    S.op("pe", f_inj)

    def stv(jblk, slot):
        v = t["PS7"][slot * 64:(slot + 1) * 64, jblk * 16:(jblk + 1) * 16]
        return v.rearrange("p (pr s2) -> p pr s2", s2=2)[:, :, slot:slot + 1] \
            .broadcast_to([64, 8, 64])

    def st3(a, r0, r1, c0, c1):
        return a[r0:r1, c0:c1].rearrange("p (pr dv) -> p pr dv", dv=64)

    def f_st1(c, s=s):
        c.w("pe")
        e = c.e
        ins = None
        for slot in range(2):
            r0, r1 = slot * 64, slot * 64 + 64
            c.i(e.tensor_mul(st3(t["Tst"], r0, r1, 0, 512), st3(P, r0, r1, 0, 512), stv(0, slot)) if e else None, "dve", 1)
            c.i(e.tensor_mul(st3(t["Tst"], r0, r1, 512, 1024), st3(Q, r0, r1, 0, 512), stv(1, slot)) if e else None, "dve", 1)
    S.op("dve", f_st1)

    def f_st2(c, s=s):
        c.w("dve")
        e = c.e
        ins = None
        c.s("dve", e.tensor_add(t["Tst"][:, 0:512], t["Tst"][:, 0:512], t["Tst"][:, 512:1024]) if e else None)
        c.i(e.tensor_add(Pn[:], t["Tst"][:, 0:512], t["PS4"][:, 0:512]) if e else None, "dve", 1)
    S.op("dve", f_st2)

    def f_st3(c, s=s):
        c.w("dve")
        e = c.e
        ins = None
        for slot in range(2):
            r0, r1 = slot * 64, slot * 64 + 64
            c.i(e.tensor_mul(st3(t["Tst"], r0, r1, 0, 512), st3(P, r0, r1, 0, 512), stv(2, slot)) if e else None, "dve", 1)
            c.i(e.tensor_mul(st3(t["Tst"], r0, r1, 512, 1024), st3(Q, r0, r1, 0, 512), stv(3, slot)) if e else None, "dve", 1)
    S.op("dve", f_st3)

    def f_st4(c, s=s):
        c.w("dve")
        e = c.e
        ins = None
        c.s("dve", e.tensor_add(t["Tst"][:, 0:512], t["Tst"][:, 0:512], t["Tst"][:, 512:1024]) if e else None)
        c.i(e.tensor_add(Qn[:], t["Tst"][:, 0:512], t["PS5"][:, 0:512]) if e else None, "dve", 1)
    S.op("dve", f_st4)


def _ph_opost(S, nc, ap, t, ci):
    orr = t["orow"][0:1, :].rearrange("p (s j) -> p s j", j=64)

    def zview(fs, s0, s1):
        return t["zsto"][:, :].rearrange("p (s two k) -> p s two k", two=2, k=16)[:, s0:s1, fs, :]

    def coefview(j, s0, s1):
        return orr[:, s0:s1, j * 16:(j + 1) * 16]

    def yv(sto, s0, s1):
        return t[sto][:, :].rearrange("p (s k) -> p s k", k=16)[:, s0:s1, :]

    pairs = [
        (lambda s0, s1: coefview(0, s0, s1), lambda s0, s1: yv("UFq", s0, s1)),
        (lambda s0, s1: coefview(1, s0, s1), lambda s0, s1: yv("USq", s0, s1)),
        (lambda s0, s1: coefview(2, s0, s1), lambda s0, s1: zview(0, s0, s1)),
        (lambda s0, s1: coefview(3, s0, s1), lambda s0, s1: zview(1, s0, s1)),
    ]
    for pi, (cf, df) in enumerate(pairs):
        def f_b(c, pi=pi, cf=cf):
            c.w("dve")
            if pi > 0:
                c.w("dve")
            e = c.e
            ins = None
            if e:
                ins = e.matmul(t["PS4"][:, 0:512], t["onesrb"][0:1, :], cf(0, 32), start=True, stop=True, skip_group_check=True)
                ins = e.matmul(t["PS5"][:, 0:512], t["onesrb"][0:1, :], cf(32, 64), start=True, stop=True, skip_group_check=True)
            c.i(ins, "pe", 1)
        S.op("pe", f_b)

        def f_m(c, pi=pi, df=df):
            c.w("pe")
            e = c.e
            ins = None
            d0 = df(0, 32) if e else None
            d1 = df(32, 64) if e else None
            ps4 = t["PS4"][:, 0:512].rearrange("p (s k) -> p s k", k=16)
            ps5 = t["PS5"][:, 0:512].rearrange("p (s k) -> p s k", k=16)
            if pi == 0:
                c.i(e.tensor_mul(t["Tst"][:, 0:512].rearrange("p (s k) -> p s k", k=16), d0, ps4) if e else None, "dve", 1)
                c.i(e.tensor_mul(t["Tst"][:, 512:1024].rearrange("p (s k) -> p s k", k=16), d1, ps5) if e else None, "dve", 1)
            else:
                c.i(e.tensor_mul(t["Bev"][:, 0:512].rearrange("p (s k) -> p s k", k=16), d0, ps4) if e else None, "dve", 1)
                c.s("dve", e.tensor_mul(t["Bev"][:, 512:1024].rearrange("p (s k) -> p s k", k=16), d1, ps5) if e else None)
                c.i(e.tensor_add(t["Tst"][:, 0:1024], t["Tst"][:, 0:1024], t["Bev"][:, 0:1024]) if e else None, "dve", 1)
        S.op("dve", f_m)

    def f_sq(c):
        c.w("dve")
        c.i(c.e.activation(t["Bev"][:, 0:1024], t["Tst"][:, 0:1024], AF.Square) if c.e else None, "act", 1)
    S.op("act", f_sq)

    def f_smm(c):
        c.w("act")
        e = c.e
        ins = None
        if e:
            ins = e.matmul(t["PS6"][0:1, 0:512], t["onesc"][:], t["Bev"][:, 0:512], start=True, stop=True, skip_group_check=True)
            ins = e.matmul(t["PS2"][0:1, 0:512], t["onesc"][:], t["Bev"][:, 512:1024], start=True, stop=True, skip_group_check=True)
        c.i(ins, "pe", 1)
    S.op("pe", f_smm)

    def f_rsq(c):
        c.w("pe")
        e = c.e
        ins = None
        c.s("act", e.activation(t["rows"][0:1, 0:512], t["PS6"][0:1, 0:512], AF.Ln, scale=1.0 / 64, bias=EPS_ON) if e else None)
        c.i(e.activation(t["rows"][0:1, 0:512], t["rows"][0:1, 0:512], AF.Exp, scale=-0.5) if e else None, "act", 1)
        c.s("act", e.activation(t["rows"][0:1, 512:1024], t["PS2"][0:1, 0:512], AF.Ln, scale=1.0 / 64, bias=EPS_ON) if e else None)
        c.i(e.activation(t["rows"][0:1, 512:1024], t["rows"][0:1, 512:1024], AF.Exp, scale=-0.5) if e else None, "act", 1)
    S.op("act", f_rsq)

    def f_gbc(c):
        c.w("act")
        e = c.e
        ins = None
        if e:
            ins = e.matmul(t["PS4"][:, 0:512], t["onesr"][0:1, :], t["rows"][0:1, 0:512], start=True, stop=True, skip_group_check=True)
            ins = e.matmul(t["PS5"][:, 0:512], t["onesr"][0:1, :], t["rows"][0:1, 512:1024], start=True, stop=True, skip_group_check=True)
        c.i(ins, "pe", 1)
    S.op("pe", f_gbc)

    def f_gate(c):
        c.w("pe")
        e = c.e
        ins = None
        if e:
            ins = e.activation(t["Bev"][:, 0:1024], t["gsto"][:, 0:1024], AF.Silu)
        c.i(ins, "act", 1)
    S.op("act", f_gate)

    def f_gate2(c):
        c.w("act")
        e = c.e
        ins = None
        c.s("dve", e.tensor_scalar_mul(t["Bev"][:, 0:1024], t["Bev"][:, 0:1024], t["onwc"][:, 0:1]) if e else None)
        c.i(e.tensor_mul(t["Bev"][:, 0:512], t["Bev"][:, 0:512], t["PS4"][:, 0:512]) if e else None, "dve", 1)
        c.i(e.tensor_mul(t["Bev"][:, 512:1024], t["Bev"][:, 512:1024], t["PS5"][:, 0:512]) if e else None, "dve", 1)
    S.op("dve", f_gate2)

    def f_og(c):
        c.w("dve")
        e = c.e
        ins = None
        if e:
            dstv = t["ogst"][:, :].rearrange("p (n hh u) -> p n hh u", n=NCH, hh=16, u=CH)[:, ci, :, :]
            src = t["Tst"][:, 0:1024].rearrange("p (u hh) -> p hh u", hh=16)
            gsrc = t["Bev"][:, 0:1024].rearrange("p (u hh) -> p hh u", hh=16)
            ins = e.tensor_mul(dstv, src, gsrc)
        c.i(ins, "dve", 1)
    S.op("dve", f_og)


def _ph_wo(S, nc, ap, t):
    for dc in range(8):
        for h in range(16):
            si = dc * 16 + h
            par = si % 2

            def f_wld(c, dc=dc, h=h, par=par, si=si):
                if si >= 2:
                    c.w("pe", back=1)
                ins = None
                if c.e:
                    src = ap["wbf"][32 + h // 2:32 + h // 2 + 1, :] \
                        .rearrange("a (p f) -> (a p) f", p=128)[(h % 2) * 64:(h % 2) * 64 + 64, dc * 128:(dc + 1) * 128]
                    pb = ((h // 2) % 2) * 64
                    ins = c.e.dma_start(out=t["wbuf"][pb:pb + 64, par * 1024:par * 1024 + 128], in_=src)
                c.i(ins, "dmA" if par == 0 else "dmB", 16)
            S.op("sp", f_wld)

            def f_mm(c, dc=dc, h=h, par=par, si=si):
                c.w("dmA" if par == 0 else "dmB")
                if h == 0 and dc > 0:
                    c.w("act")
                ins = None
                if c.e:
                    pb = ((h // 2) % 2) * 64
                    rhs = t["ogst"][pb:pb + 64, :] \
                        .rearrange("p (n hh u) -> p n hh u", n=NCH, hh=16, u=CH)[:, :, h, :]
                    ins = c.e.matmul(t["PS3"][:, 0:256], t["wbuf"][pb:pb + 64, par * 1024:par * 1024 + 128],
                                     rhs, start=(h == 0), stop=(h == 15), skip_group_check=True)
                c.i(ins, "pe", 1)
            S.op("pe", f_mm)

        def f_ev(c, dc=dc):
            c.w("pe")
            ins = None
            if c.e:
                ins = c.e.activation(t["stage"][:, 0:256], t["PS3"][:, 0:256], AF.Identity,
                                     bias=t["b_fm"][:, dc * 5 + 4:dc * 5 + 5])
            c.i(ins, "act", 1)
        S.op("act", f_ev)

        def f_res(c, dc=dc):
            c.w("act")
            ins = None
            if c.e:
                ins = c.e.tensor_add(t["xfm"][:, dc * TILE:(dc + 1) * TILE],
                                     t["xfm"][:, dc * TILE:(dc + 1) * TILE], t["stage"][:, 0:256])
            c.i(ins, "dve", 1)
        S.op("dve", f_res)

        def f_out(c, dc=dc):
            c.w("dve")
            ins = None
            if c.e:
                dst = ap["x1T"][dc * 128:(dc + 1) * 128, :].rearrange("p (n u) -> p n u", u=TILE)
                ins = c.e.dma_start(out=dst[:, c.iv["ti"], :], in_=t["xfm"][:, dc * TILE:(dc + 1) * TILE])
            c.i(ins, "dma", 16)
        S.op("sp", f_out)

    def f_fin(c):
        if c.e:
            c.w("dma")
        else:
            c.w("dma")
    S.op("sp", f_fin)



W_POOL = [("pj_k", 4096), ("pj_q", 4096), ("pj_v", 4096), ("pj_g", 4096),
          ("klhs", 4096), ("zsto", 4096), ("UFq", 2048), ("USq", 2048),
          ("wbuf", 2048), ("vcol", 2048)]
BF16_NATIVE = {"xn", "ogst", "wbuf"}
XQ_SEGS = {
    0: [("xn", 0, 2048, 0)],
    1: [("ktr", 0, 2048, 0)],
    2: [("Bev", 0, 2048, 0)],
    3: [("ogst", 0, 2048, 0)],
    4: [("ogst", 0, 2048, 2048)],
    5: [("Tst", 0, 2048, 0)],
    6: [("st_P0", 0, 1024, 0), ("st_Q0", 1024, 1024, 0)],
    7: [("st_P1", 0, 1024, 0), ("st_Q1", 1024, 1024, 0)],
}


def _bf(t, nm):
    a = t[nm][:]
    return a if nm in BF16_NATIVE else a.bitcast(BF16)


def _wpool_tiles(t):
    tiles = []
    for nm, ncols in W_POOL:
        for off in range(0, ncols, 128):
            tiles.append((nm, off))
    assert len(tiles) == 256
    return tiles


def _xq_ap(t, dc, col0, ncols):
    for nm, base, n, tc0 in XQ_SEGS[dc]:
        if col0 >= base and col0 + ncols <= base + n:
            o = tc0 + col0 - base
            return _bf(t, nm)[:, o:o + ncols]
    raise AssertionError((dc, col0, ncols))


def _wq_prep(S, nc, ap, t, wsrc, meanslot, WT):
    is_w1 = wsrc == "w1T"
    nrow = 8 if is_w1 else 32
    wcols = 4096 if is_w1 else 1024
    for r in range(nrow):
        for half in range(wcols // 2048 if wcols >= 2048 else 1):
            ncol = min(2048, wcols)

            def f_ld(c, r=r, half=half, ncol=ncol):
                if r + half > 0:
                    c.w("act", back=0)
                ins = None
                if c.e:
                    ins = c.e.dma_start(out=t["stage"][:, 0:ncol],
                                        in_=ap[wsrc][r * 128:(r + 1) * 128, half * 2048:half * 2048 + ncol])
                c.i(ins, "dma", 16)
            S.op("sp", f_ld)

            def f_abs(c, r=r, half=half, ncol=ncol, idx=r * (wcols // 2048 if wcols >= 2048 else 1) + half):
                c.w("dma")
                ins = None
                if c.e:
                    ins = c.e.activation(t["stage"][:, 0:ncol], t["stage"][:, 0:ncol], AF.Abs,
                                         accum_out=t["gsto"][:, idx:idx + 1])
                c.i(ins, "act", 1)
            S.op("act", f_abs)
    nacc = nrow * (wcols // 2048 if wcols >= 2048 else 1)

    def f_mean(c, nacc=nacc):
        c.w("act")
        e = c.e
        ins = None
        if e:
            ins = e.reduce_sum(t["gsto"][:, 40:41], t["gsto"][:, 0:nacc], axis=mybir.AxisListType.X)
        c.i(ins, "dve", 1)
    S.op("dve", f_mean)

    def f_mean2(c):
        c.w("dve")
        c.i(c.e.matmul(t["PS6"][0:1, 0:1], t["gsto"][:, 40:41], t["onesc"][:],
                       start=True, stop=True, skip_group_check=True) if c.e else None, "pe", 1)
    S.op("pe", f_mean2)

    def f_mean3(c, meanslot=meanslot):
        c.w("pe")
        e = c.e
        ins = None
        c.s("dve", e.tensor_scalar(t["crow"][0:1, 0:1], t["PS6"][0:1, 0:1],
                                   1.0 / (4096.0 * 1024.0), 1e-5, ALU.mult, ALU.max) if e else None)
        c.i(e.reciprocal(t["crow"][0:1, 1:2], t["crow"][0:1, 0:1]) if e else None, "dve", 1)
    S.op("dve", f_mean3)

    def f_mbc(c, meanslot=meanslot):
        c.w("dve")
        e = c.e
        ins = None
        if e:
            ins = e.matmul(t["PS6"][:, 2:4], t["onesr"][0:1, :], t["crow"][0:1, 0:2],
                           start=True, stop=True, skip_group_check=True)
        c.i(ins, "pe", 1)
    S.op("pe", f_mbc)

    def f_mcp(c, meanslot=meanslot):
        c.w("pe")
        e = c.e
        ins = None
        c.i(e.tensor_copy(t["rfc"][:, meanslot:meanslot + 1], t["PS6"][:, 2:3]) if e else None, "dve", 1)
        c.i(e.tensor_copy(t["rfc"][:, meanslot + 1:meanslot + 2], t["PS6"][:, 3:4]) if e else None, "dve", 1)
    S.op("dve", f_mcp)
    S.barrier()

    for r in range(nrow):
        nhalf = wcols // 2048 if wcols >= 2048 else 1
        for half in range(nhalf):
            ncol = min(2048, wcols)

            def f_ld(c, r=r, half=half, ncol=ncol):
                c.w("dve", back=0)
                c.w("gp", back=0)
                ins = None
                if c.e:
                    ins = c.e.dma_start(out=t["stage"][:, 0:ncol],
                                        in_=ap[wsrc][r * 128:(r + 1) * 128, half * 2048:half * 2048 + ncol])
                c.i(ins, "dma", 16)
            S.op("sp", f_ld)

            def f_q(c, r=r, half=half, ncol=ncol):
                c.w("dma")
                e = c.e
                sw = t["rfc"][:, 6:7]
                c.s("dve", e.tensor_scalar(t["stage"][:, 0:ncol], t["stage"][:, 0:ncol],
                                           sw, MAGIC, ALU.mult, ALU.add) if e else None)
                c.i(e.tensor_scalar(t["stage"][:, 0:ncol], t["stage"][:, 0:ncol],
                                    MAGIC, 1.0, ALU.subtract, ALU.min) if e else None, "dve", 1)
            S.op("dve", f_q)

            def f_st(c, r=r, half=half, ncol=ncol):
                c.w("dve")
                e = c.e
                for j in range(ncol // 128):
                    colg = half * 2048 + j * 128
                    if is_w1:
                        ti = r * 32 + colg // 128
                    else:
                        ti = r * 8 + colg // 128
                    nm, off = WT[ti]
                    dst = _bf(t, nm)[:, off:off + 128]
                    c.i(e.tensor_scalar_max(dst, t["stage"][:, j * 128:(j + 1) * 128], -1.0) if e else None, "gp", 1)
            S.op("gp", f_st)
    S.barrier()


def _ph_ffn(S, nc, ap, t):
    WT = _wpool_tiles(t)

    for fi in range(NTILE):
        def fa_x(c, fi=fi):
            c.w("dve"); c.w("gp")
            for cc in range(8):
                ins = None
                if c.e:
                    src = ap["x1T"][cc * 128:(cc + 1) * 128, fi * TILE:(fi + 1) * TILE]
                    ins = c.e.dma_start(out=t["xfm"][:, cc * TILE:(cc + 1) * TILE], in_=src)
                c.i(ins, "dma", 16)
        S.op("sp", fa_x)

        for cc in range(8):
            def f_sq(c, cc=cc):
                c.w("dma"); c.w("pe")
                c.i(c.e.activation(t["sqs"][:], t["xfm"][:, cc * TILE:(cc + 1) * TILE], AF.Square) if c.e else None, "act", 1)
            S.op("act", f_sq)

            def f_mm(c, cc=cc):
                c.w("dma"); c.w("act")
                e = c.e
                ins = None
                if e:
                    e.matmul(t["PS6"][0:1, 0:256], t["onesc"][:], t["xfm"][:, cc * TILE:(cc + 1) * TILE],
                             start=(cc == 0), stop=(cc == 7), skip_group_check=True)
                    ins = e.matmul(t["PS6"][0:1, 256:512], t["onesc"][:], t["sqs"][:],
                                   start=(cc == 0), stop=(cc == 7), skip_group_check=True)
                c.i(ins, "pe", 1)
            S.op("pe", f_mm)

        def f_lnr(c):
            c.w("pe")
            e = c.e
            c.i(e.tensor_scalar_mul(t["rows"][0:1, 0:256], t["PS6"][0:1, 0:256], 1.0 / 1024) if e else None, "dve", 1)
            c.s("dve", e.tensor_scalar_mul(t["rows"][0:1, 256:512], t["PS6"][0:1, 256:512], 1.0 / 1024) if e else None)
            c.s("dve", e.tensor_mul(t["rows"][0:1, 512:768], t["rows"][0:1, 0:256], t["rows"][0:1, 0:256]) if e else None)
            c.i(e.tensor_sub(t["rows"][0:1, 768:1024], t["rows"][0:1, 256:512], t["rows"][0:1, 512:768]) if e else None, "dve", 1)
        S.op("dve", f_lnr)

        def f_lna(c):
            c.w("dve")
            e = c.e
            c.s("act", e.activation(t["rows"][0:1, 1024:1280], t["rows"][0:1, 768:1024], AF.Ln, bias=EPS_LN) if e else None)
            c.i(e.activation(t["rows"][0:1, 1280:1536], t["rows"][0:1, 1024:1280], AF.Exp, scale=-0.5) if e else None, "act", 1)
        S.op("act", f_lna)

        def f_tree(c):
            c.w("dma")
            e = c.e
            c.s("dve", e.tensor_max(t["sqs"][:], t["xfm"][:, 0:256], t["xfm"][:, 256:512]) if e else None)
            for cc in range(2, 8):
                ins2 = e.tensor_max(t["sqs"][:], t["sqs"][:], t["xfm"][:, cc * 256:(cc + 1) * 256]) if e else None
                (c.s("dve", ins2) if cc < 7 else c.i(ins2, "dve", 1))
        S.op("dve", f_tree)

        def f_tree2(c):
            c.w("dma")
            e = c.e
            c.s("dve", e.tensor_tensor(t["stage"][:, 0:256], t["xfm"][:, 0:256], t["xfm"][:, 256:512], ALU.min) if e else None)
            for cc in range(2, 8):
                ins2 = e.tensor_tensor(t["stage"][:, 0:256], t["stage"][:, 0:256],
                                       t["xfm"][:, cc * 256:(cc + 1) * 256], ALU.min) if e else None
                (c.s("dve", ins2) if cc < 7 else c.i(ins2, "dve", 1))
        S.op("dve", f_tree2)

        def f_trp(c):
            c.w("dve"); c.w("gp")
            e = c.e
            ins = None
            if e:
                for b in range(2):
                    ins = e.transpose(t["PS2"][:, b * 128:(b + 1) * 128], t["sqs"][:, b * 128:(b + 1) * 128], t["ident"][:])
                for b in range(2):
                    ins = e.transpose(t["PS3"][:, b * 128:(b + 1) * 128], t["stage"][:, b * 128:(b + 1) * 128], t["ident"][:])
            c.i(ins, "pe", 1)
        S.op("pe", f_trp)

        def f_red(c):
            c.w("pe")
            e = c.e
            for b in range(2):
                c.i(e.reduce_max(t["UFk"][:, b:b + 1], t["PS2"][:, b * 128:(b + 1) * 128], axis=mybir.AxisListType.X) if e else None, "dve", 1)
                c.i(e.tensor_reduce(t["UFk"][:, 2 + b:3 + b], t["PS3"][:, b * 128:(b + 1) * 128],
                                    axis=mybir.AxisListType.X, op=ALU.min) if e else None, "dve", 1)
        S.op("dve", f_red)

        def f_n1t(c):
            c.w("dve")
            e = c.e
            ins = None
            if e:
                for b in range(2):
                    ins = e.transpose(t["PS6"][0:1, b * 128:(b + 1) * 128], t["UFk"][:, b:b + 1], t["ident"][:])
                    ins = e.transpose(t["PS6"][0:1, 256 + b * 128:256 + (b + 1) * 128], t["UFk"][:, 2 + b:3 + b], t["ident"][:])
            c.i(ins, "pe", 1)
        S.op("pe", f_n1t)

        def f_qrow(c):
            c.w("pe"); c.w("act")
            e = c.e
            c.i(e.tensor_sub(t["rows"][0:1, 1536:1792], t["PS6"][0:1, 0:256], t["rows"][0:1, 0:256]) if e else None, "dve", 1)
            c.s("dve", e.tensor_sub(t["rows"][0:1, 1792:2048], t["rows"][0:1, 0:256], t["PS6"][0:1, 256:512]) if e else None)
            c.s("dve", e.tensor_max(t["rows"][0:1, 1536:1792], t["rows"][0:1, 1536:1792], t["rows"][0:1, 1792:2048]) if e else None)
            c.s("dve", e.tensor_mul(t["rows"][0:1, 1536:1792], t["rows"][0:1, 1536:1792], t["rows"][0:1, 1280:1536]) if e else None)
            c.i(e.tensor_scalar_max(t["rows"][0:1, 1536:1792], t["rows"][0:1, 1536:1792], 1e-5) if e else None, "dve", 1)
        S.op("dve", f_qrow)

        def f_qact(c):
            c.w("dve")
            e = c.e
            c.s("act", e.activation(t["rows"][0:1, 1792:2048], t["rows"][0:1, 1536:1792], AF.Ln) if e else None)
            c.i(e.activation(t["rows"][0:1, 1792:2048], t["rows"][0:1, 1792:2048], AF.Exp,
                             scale=-1.0, bias=float(np.log(127.0))) if e else None, "act", 1)
        S.op("act", f_qact)

        def f_qrow2(c, fi=fi):
            c.w("act")
            e = c.e
            c.s("dve", e.tensor_mul(t["rows"][0:1, 1024:1280], t["rows"][0:1, 1280:1536], t["rows"][0:1, 1792:2048]) if e else None)
            c.i(e.tensor_mul(t["rows"][0:1, 768:1024], t["rows"][0:1, 0:256], t["rows"][0:1, 1024:1280]) if e else None, "dve", 1)
            c.i(e.tensor_copy(t["orow"][:].bitcast(F32)[0:1, fi * 256:(fi + 1) * 256], t["rows"][0:1, 1536:1792]) if e else None, "dve", 1)
        S.op("dve", f_qrow2)

        def f_qbc(c):
            c.w("dve")
            c.i(c.e.matmul(t["PS7"][:, 0:512], t["onesr"][0:1, :], t["rows"][0:1, 768:1280],
                           start=True, stop=True, skip_group_check=True) if c.e else None, "pe", 1)
        S.op("pe", f_qbc)

        for cc in range(8):
            def f_q1(c, cc=cc):
                c.w("pe"); c.w("dve")
                c.i(c.e.tensor_mul(t["sqs"][:], t["xfm"][:, cc * TILE:(cc + 1) * TILE], t["PS7"][:, 256:512]) if c.e else None, "dve", 1)
            S.op("dve", f_q1)

            def f_q2(c, cc=cc):
                c.w("dve")
                c.i(c.e.tensor_tensor(t["stage"][:, 256:512], t["sqs"][:], t["PS7"][:, 0:256], ALU.subtract) if c.e else None, "dve", 1)
            S.op("dve", f_q2)

            def f_q3(c, cc=cc, fi=fi):
                c.w("dve")
                ins = None
                if c.e:
                    dst = _xq_ap(t, cc, fi * 256, 256)
                    ins = c.e.tensor_scalar(dst, t["stage"][:, 256:512], MAGIC, MAGIC, ALU.add, ALU.subtract)
                c.i(ins, "dve", 1)
            S.op("dve", f_q3)
    S.barrier()

    _wq_prep(S, nc, ap, t, "w1T", 5, WT)

    def f_dssc(c):
        e = c.e
        ins = None
        if e:
            ins = e.tensor_scalar(t["orow"][:].bitcast(F32)[0:1, 0:2048], t["orow"][:].bitcast(F32)[0:1, 0:2048],
                                  t["rfc"][0:1, 5:6], 1.0 / 127.0, ALU.mult, ALU.mult)
        c.i(ins, "dve", 1)
    S.op("dve", f_dssc)
    S.barrier()

    def f_z(c):
        e = c.e
        c.i(e.memset(t["gsto"][:], 0.0) if e else None, "gp", 1)
        c.i(e.memset(t["ffnx"][:], 0.0) if e else None, "gp", 1)
    S.op("gp", f_z)

    def f_bds(c):
        e = c.e
        ins = None
        if e:
            for g in range(4):
                ins = e.matmul(t[f"PS{4 + g}"][:, 0:512], t["onesr"][0:1, :],
                               t["orow"][:].bitcast(F32)[0:1, g * 512:(g + 1) * 512],
                               start=True, stop=True, skip_group_check=True)
        c.i(ins, "pe", 1)
    S.op("pe", f_bds)
    S.barrier()

    for hc in range(32):
        def f_mm(c, hc=hc):
            c.w("dve", back=0)
            e = c.e
            ins = None
            if e:
                for dc in range(8):
                    nm, off = WT[dc * 32 + hc]
                    lhs = _bf(t, nm)[:, off:off + 128]
                    for g in range(4):
                        ins = e.matmul(t[f"PS{g}"][:, 0:512], lhs, _xq_ap(t, dc, g * 512, 512),
                                       start=(dc == 0), stop=(dc == 7), skip_group_check=True)
            c.i(ins, "pe", 1)
        S.op("pe", f_mm)

        def f_ev(c, hc=hc):
            c.w("pe")
            c.w("dma")
            e = c.e
            for g in range(4):
                ins2 = e.tensor_copy(t["stage"][:, g * 512:(g + 1) * 512], t[f"PS{g}"][:, 0:512]) if e else None
                (c.s("dve", ins2) if g == 3 else c.i(ins2, "dve", 1))
            for g in range(4):
                c.i(e.tensor_mul(t["stage"][:, g * 512:(g + 1) * 512],
                                 t["stage"][:, g * 512:(g + 1) * 512], t[f"PS{4 + g}"][:, 0:512]) if e else None, "dve", 1)
        S.op("dve", f_ev)

        def f_conv(c, hc=hc):
            c.w("dve")
            e = c.e
            w0 = t["cwv"][:, hc * 3:hc * 3 + 1]
            w1_ = t["cwv"][:, hc * 3 + 1:hc * 3 + 2]
            w2_ = t["cwv"][:, hc * 3 + 2:hc * 3 + 3]
            cb = t["cbv"][:, hc:hc + 1]
            c.i(e.memset(t["xfm"][:, 0:1], 0.0) if e else None, "dve", 1)
            c.s("dve", e.tensor_scalar_mul(t["xfm"][:, 1:2048], t["stage"][:, 0:2047], w0) if e else None)
            c.s("dve", e.scalar_tensor_tensor(t["xfm"][:, 1:2048], t["stage"][:, 1:2048], w1_,
                                              t["xfm"][:, 1:2048], ALU.mult, ALU.add) if e else None)
            c.s("dve", e.scalar_tensor_tensor(t["xfm"][:, 0:2047], t["stage"][:, 1:2048], w2_,
                                              t["xfm"][:, 0:2047], ALU.mult, ALU.add) if e else None)
            c.i(e.tensor_scalar_add(t["xfm"][:, 0:2048], t["xfm"][:, 0:2048], cb) if e else None, "dve", 1)
        S.op("dve", f_conv)

        def f_conv0(c, hc=hc):
            c.w("dve")
            e = c.e
            ins = None
            if e:
                w1_ = t["cwv"][:, hc * 3 + 1:hc * 3 + 2]
                ins = e.scalar_tensor_tensor(t["xfm"][:, 0:1], t["stage"][:, 0:1], w1_,
                                             t["xfm"][:, 0:1], ALU.mult, ALU.add)
            c.i(ins, "dve", 1)
        S.op("dve", f_conv0)

        def f_silu(c, hc=hc):
            c.w("dve")
            e = c.e
            ins = None
            if e:
                dst = _bf(t, "stage")[:, 0:2048]
                ins = e.activation(dst, t["xfm"][:, 0:2048], AF.Silu)
            c.i(ins, "act", 1)
        S.op("act", f_silu)

        def f_abs(c, hc=hc):
            c.w("act")
            e = c.e
            ins = None
            if e:
                ins = e.activation(t["xfm"][:, 0:2048], _bf(t, "stage")[:, 0:2048], AF.Abs)
            c.i(ins, "act", 1)
        S.op("act", f_abs)

        def f_m2(c, hc=hc):
            c.w("act")
            e = c.e
            c.i(e.tensor_max(t["gsto"][:], t["gsto"][:], t["xfm"][:, 0:1024]) if e else None, "dve", 1)
            c.i(e.tensor_max(t["ffnx"][:], t["ffnx"][:], t["xfm"][:, 1024:2048]) if e else None, "dve", 1)
        S.op("dve", f_m2)

        def f_hout(c, hc=hc):
            c.w("act")
            ins = None
            if c.e:
                ins = c.e.dma_start(out=ap["hsT"][hc * 128:(hc + 1) * 128, :], in_=_bf(t, "stage")[:, 0:2048])
            c.i(ins, "dma", 16)
        S.op("sp", f_hout)

        def f_hwait(c, hc=hc):
            c.w("dma")
        S.op("sp", f_hwait)
    S.barrier()


    for b in range(16):
        def f_t(c, b=b):
            if b > 0:
                c.w("dve")
            ins = None
            if c.e:
                src = t["gsto"][:, (b % 8) * 128:(b % 8) * 128 + 128] if b < 8 else t["ffnx"][:, (b - 8) * 128:(b - 7) * 128]
                ins = c.e.transpose(t["PS2"][:, 0:128], src, t["ident"][:])
            c.i(ins, "pe", 1)
        S.op("pe", f_t)

        def f_r(c, b=b):
            c.w("pe")
            ins = None
            if c.e:
                ins = c.e.reduce_max(t["USk"][:, b:b + 1], t["PS2"][:, 0:128], axis=mybir.AxisListType.X)
            c.i(ins, "dve", 1)
        S.op("dve", f_r)
    for b in range(16):
        def f_t2(c, b=b):
            c.w("dve")
            if b % 4 == 0 and b > 0:
                c.w("act")
            ins = None
            if c.e:
                ins = c.e.transpose(t["PS6"][0:1, (b % 4) * 128:(b % 4) * 128 + 128],
                                    t["USk"][:, b:b + 1], t["ident"][:])
            c.i(ins, "pe", 1)
        S.op("pe", f_t2)
        if b % 4 == 3:
            def f_ev2(c, b=b):
                c.w("pe")
                ins = None
                if c.e:
                    ins = c.e.copy(t["rows"][0:1, (b - 3) * 128:(b + 1) * 128], t["PS6"][0:1, 0:512])
                c.i(ins, "act", 1)
            S.op("act", f_ev2)

    def f_s2(c):
        c.w("act")
        e = c.e
        ins = None
        if e:
            ins = e.tensor_scalar_max(t["rows"][0:1, 0:2048], t["rows"][0:1, 0:2048], 1e-5)
        c.i(ins, "dve", 1)
    S.op("dve", f_s2)

    def f_s2a(c):
        c.w("dve")
        e = c.e
        c.s("act", e.activation(t["orow"][:].bitcast(F32)[0:1, 0:2048], t["rows"][0:1, 0:2048], AF.Ln) if e else None)
        c.i(e.activation(t["orow"][:].bitcast(F32)[0:1, 0:2048], t["orow"][:].bitcast(F32)[0:1, 0:2048], AF.Exp,
                         scale=-1.0, bias=float(np.log(127.0))) if e else None, "act", 1)
    S.op("act", f_s2a)

    def f_s2b(c):
        c.w("act")
        e = c.e
        ins = None
        if e:
            ins = e.tensor_scalar_mul(t["rows"][0:1, 0:2048], t["rows"][0:1, 0:2048], 1.0 / 127.0)
        c.i(ins, "dve", 1)
    S.op("dve", f_s2b)
    S.barrier()

    _wq_prep(S, nc, ap, t, "w2T", 5, WT)

    def f_ds2sc(c):
        e = c.e
        ins = None
        if e:
            ins = e.tensor_scalar_mul(t["rows"][0:1, 0:2048], t["rows"][0:1, 0:2048], t["rfc"][0:1, 5:6])
        c.i(ins, "dve", 1)
    S.op("dve", f_ds2sc)
    S.barrier()

    for ti4 in range(4):
        def f_bs(c, ti4=ti4):
            c.w("dve"); c.w("gp")
            e = c.e
            ins = None
            if e:
                ins = e.matmul(t["PS0"][:, 0:512], t["onesr"][0:1, :],
                               t["orow"][:].bitcast(F32)[0:1, ti4 * 512:(ti4 + 1) * 512],
                               start=True, stop=True, skip_group_check=True)
                ins = e.matmul(t["PS1"][:, 0:512], t["onesr"][0:1, :],
                               t["rows"][0:1, ti4 * 512:(ti4 + 1) * 512],
                               start=True, stop=True, skip_group_check=True)
            c.i(ins, "pe", 1)
        S.op("pe", f_bs)

        def f_bsc(c, ti4=ti4):
            c.w("pe")
            e = c.e
            c.i(e.tensor_copy(_bf(t, "Tst")[:, 0:512], t["PS0"][:, 0:512]) if e else None, "dve", 1)
            c.i(e.tensor_copy(t["Tst"][:, 512:1024], t["PS1"][:, 0:512]) if e else None, "dve", 1)
        S.op("dve", f_bsc)

        for hc in range(32):
            def f_hld(c, hc=hc, ti4=ti4):
                c.w("dve", back=0)
                ins = None
                if c.e:
                    ins = c.e.dma_start(out=_bf(t, "xn")[:, 512:1024],
                                        in_=ap["hsT"][hc * 128:(hc + 1) * 128, ti4 * 512:(ti4 + 1) * 512])
                c.i(ins, "dma", 16)
            S.op("sp", f_hld)

            def f_hq(c, hc=hc, ti4=ti4):
                c.w("dma")
                c.w("pe")
                e = c.e
                c.s("dve", e.tensor_mul(t["stage"][:, 0:512],
                                        _bf(t, "xn")[:, 512:1024], _bf(t, "Tst")[:, 0:512]) if e else None)
                c.i(e.tensor_scalar(_bf(t, "xn")[:, 0:512], t["stage"][:, 0:512],
                                    MAGIC, MAGIC, ALU.add, ALU.subtract) if e else None, "dve", 1)
            S.op("dve", f_hq)

            def f_ymm(c, hc=hc, ti4=ti4):
                c.w("dve")
                e = c.e
                ins = None
                if e:
                    for dc in range(8):
                        nm, off = WT[hc * 8 + dc]
                        lhs = _bf(t, nm)[:, off:off + 128]
                        ins = e.matmul(t[f"PS{dc}"][:, 0:512], lhs, _bf(t, "xn")[:, 0:512],
                                       start=(hc == 0), stop=(hc == 31), skip_group_check=True)
                c.i(ins, "pe", 1)
            S.op("pe", f_ymm)

        for dc in range(8):
            def f_x1ld(c, dc=dc, ti4=ti4):
                if dc == 0:
                    c.w("pe")
                c.w("dve", back=0)
                ins = None
                if c.e:
                    ins = c.e.dma_start(out=t["xfm"][:, (dc % 4) * 512:(dc % 4) * 512 + 512],
                                        in_=ap["x1T"][dc * 128:(dc + 1) * 128, ti4 * 512:(ti4 + 1) * 512])
                c.i(ins, "dma", 16)
            S.op("sp", f_x1ld)

            def f_yev(c, dc=dc, ti4=ti4):
                c.w("pe"); c.w("dma")
                e = c.e
                c.s("dve", e.tensor_mul(t["stage"][:, 0:512], t[f"PS{dc}"][:, 0:512], t["Tst"][:, 512:1024]) if e else None)
                c.i(e.tensor_add(t["stage"][:, 512:1024], t["stage"][:, 0:512],
                                 t["xfm"][:, (dc % 4) * 512:(dc % 4) * 512 + 512]) if e else None, "dve", 1)
            S.op("dve", f_yev)

            def f_yout(c, dc=dc, ti4=ti4):
                c.w("dve")
                ins = None
                if c.e:
                    ins = c.e.dma_start(out=ap["outT"][dc * 128:(dc + 1) * 128, ti4 * 512:(ti4 + 1) * 512],
                                        in_=t["stage"][:, 512:1024])
                c.i(ins, "dma", 16)
            S.op("sp", f_yout)
    S.barrier()

    def f_fin(c):
        c.w("dma")
    S.op("sp", f_fin)



_CACHE = {}


def kernel(**inputs):
    x = np.asarray(inputs["L0_attn_hidden_states"], np.float32)
    cm = {
        "Wq": "L0_attn_Wq", "bq": "L0_attn_bq", "Wk": "L0_attn_Wk", "bk": "L0_attn_bk",
        "Wv": "L0_attn_Wv", "bv": "L0_attn_bv", "Wb": "L0_attn_Wb", "bb": "L0_attn_bb",
        "Wfd": "L0_attn_Wfd", "bfd": "L0_attn_bfd", "fdb": "L0_attn_fast_decay_bias",
        "Wsd": "L0_attn_Wsd", "bsd": "L0_attn_bsd", "sdb": "L0_attn_slow_decay_bias",
        "Wfg": "L0_attn_Wfg", "bfg": "L0_attn_bfg", "Wsg": "L0_attn_Wsg", "bsg": "L0_attn_bsg",
        "Wg": "L0_attn_Wg", "bg": "L0_attn_bg", "onw": "L0_attn_onorm_w",
        "Wo": "L0_attn_Wo", "bo": "L0_attn_bo", "convb": "L0_ffn_conv_b",
    }
    com = {k: np.ascontiguousarray(np.asarray(inputs[v], np.float32)) for k, v in cm.items()}
    com["w1T"] = np.ascontiguousarray(np.asarray(inputs["L0_ffn_w1"], np.float32).T)
    com["w2T"] = np.ascontiguousarray(np.asarray(inputs["L0_ffn_w2"], np.float32).T)
    com["convw"] = np.ascontiguousarray(np.asarray(inputs["L0_ffn_conv_w"], np.float32).reshape(DH, 3).T.reshape(-1))
    com["rf"] = np.asarray(inputs["L0_attn_resonance_flux"], np.float32).reshape(1)
    prog = _CACHE.get("p")
    if prog is None:
        prog = build_program()
        _CACHE["p"] = prog
    in_maps = []
    for b in range(NCORE):
        m = dict(com)
        m["xT"] = np.ascontiguousarray(x[b].T)
        in_maps.append(m)
    res = run_bass_kernel_spmd(prog["nc"], in_maps, core_ids=list(range(NCORE)))
    return np.stack([np.asarray(r["outT"], np.float32).T for r in res.results])
